# version 29
# baseline (speedup 1.0000x reference)
import numpy as np
import ml_dtypes
from contextlib import ExitStack
import concourse.bass as bass
import concourse.mybir as mybir
from concourse.bass_utils import run_bass_kernel_spmd

F32 = mybir.dt.float32
BF16 = mybir.dt.bfloat16
AF = mybir.ActivationFunctionType
ALU = mybir.AluOpType

D = 64
DM = 1024
DC = 8
FH = 3072
HC = 24
NEG = -1.0e30
C127 = float(np.float32(1.0) / np.float32(127.0))
EPS = 1e-5
EPS_FFN = float(1e-5 / np.float64(C127) ** 4)
F0 = np.float32(0.0)
F1 = np.float32(1.0)
FH2 = np.float32(-0.5)


class _Plan:

    def __init__(self):
        self.mode = "plan"
        self.vals = {}
        self.ctr = {}
        self.E = None
        self.sems = None

    def planning(self):
        return self.mode == "plan"

    def on(self, eng):
        return self.mode == eng

    def ms(self, eng, sem, key, inst=None, k=1):
        if self.planning():
            v = self.ctr.get(sem, 0) + k
            self.ctr[sem] = v
            assert (sem,) + key not in self.vals, ("dup milestone", sem, key)
            self.vals[(sem,) + key] = v
        elif self.on(eng):
            assert inst is not None, ("missing inst", sem, key)
            inst.then_inc(self.sems[sem], k)

    def at(self, sem, *key):
        return self.vals[(sem,) + key]

    def wt(self, eng, sem, *key):
        if self.planning():
            assert ((sem,) + key) in self.vals, ("wait on unplanned", sem, key)
        elif self.on(eng):
            self.E.wait_ge(self.sems[sem], self.vals[(sem,) + key])

    def snap(self, name):
        if self.planning():
            self.vals[("snap", name)] = dict(self.ctr)

    def fence(self, eng, name, sems):
        if self.planning():
            assert ("snap", name) in self.vals
            return
        if self.on(eng):
            snapshot = self.vals[("snap", name)]
            for s in sems:
                v = snapshot.get(s, 0)
                if v > 0:
                    self.E.wait_ge(self.sems[s], v)


def build_nc(T=4096, n_cores=8, stage=5, debug=False):
    NT = T // 128
    TH = T // 2
    NQT = TH // 128
    NB = T // 512
    NSLOT = NB // 2
    NSL = TH // 256
    NU = NQT + NT

    def csched(s):
        return (2 * s + 2) * 4

    nc = bass.Bass("TRN2", target_bir_lowering=False, debug=False)

    x_d = nc.dram_tensor("x", [T, DM], F32, kind="ExternalInput")
    xown_d = nc.dram_tensor("x_own", [TH, DM], F32, kind="ExternalInput")
    wqkv_d = nc.dram_tensor("wqkv", [DM, 2 * 1536], F32, kind="ExternalInput")
    bqkv_d = nc.dram_tensor("bqkv", [1, 2 * 1536], F32, kind="ExternalInput")
    wproj_d = nc.dram_tensor("wproj", [DM, DM], F32, kind="ExternalInput")
    bproj_d = nc.dram_tensor("bproj", [1, DM], F32, kind="ExternalInput")
    wg_d = nc.dram_tensor("wg", [DM, FH], F32, kind="ExternalInput")
    wu_d = nc.dram_tensor("wu", [DM, FH], F32, kind="ExternalInput")
    wd_d = nc.dram_tensor("wd", [FH, DM], F32, kind="ExternalInput")
    ropek_d = nc.dram_tensor("ropek", [T, 128], F32, kind="ExternalInput")
    ropeq_d = nc.dram_tensor("ropeq", [TH, 128], F32, kind="ExternalInput")
    maskb_d = nc.dram_tensor("maskb", [128, 8 * 512], F32, kind="ExternalInput")
    ident_d = nc.dram_tensor("ident", [128, 128], F32, kind="ExternalInput")

    out_d = nc.dram_tensor("out", [TH, DM], F32, kind="ExternalOutput")

    dkind = "ExternalOutput" if debug else "Internal"
    x2_d = nc.dram_tensor("x2s", [TH, DM], F32, kind=dkind)
    ln2_d = nc.dram_tensor("ln2s", [DM, TH], BF16, kind=dkind)
    wcat_d = nc.dram_tensor("wcat", [128, 3 * 24576], BF16, kind="Internal")
    if debug:
        dbg_qfm = nc.dram_tensor("dbg_qfm", [128, 4 * TH], F32, kind="ExternalOutput")
        dbg_kfm = nc.dram_tensor("dbg_kfm", [128, 4 * T], F32, kind="ExternalOutput")
        dbg_vtm = nc.dram_tensor("dbg_vtm", [128, NT * 520], F32, kind="ExternalOutput")
        dbg_ofm = nc.dram_tensor("dbg_ofm", [128, 8 * TH], F32, kind="ExternalOutput")

    SB_BASE = 16384

    def sb(name, shape, dt, off):
        return nc.alloc_sbuf_tensor_at(name, shape, dt, offset=SB_BASE + off)

    ident_s = sb("ident_s", [128, 128], BF16, 0)
    onesf_s = sb("onesf_s", [128, 64], F32, 256)
    ones_b = sb("ones_b", [128, 512], BF16, 512)
    eps_s = sb("eps_s", [128, 8], F32, 1536)
    bqkv_s = sb("bqkv_s", [1, 2 * 1536], BF16, 1568)
    bproj_s = sb("bproj_s", [1, DM], BF16, 7712)
    maskb_s = sb("maskb_s", [128, 8 * 512], BF16, 9760)
    identf_s = sb("identf_s", [128, 128], F32, 17952)
    O_KVQ = 18464

    kfm_s = sb("kfm", [128, 4 * T], BF16, O_KVQ)
    vtm_s = sb("vtm", [128, NT * 520], BF16, O_KVQ + 2 * 4 * T)
    qfm_s = sb("qfm", [128, 4 * TH], BF16, O_KVQ + 2 * 4 * T + 2 * NT * 520)
    O_LG1 = O_KVQ + 2 * 4 * T + 2 * NT * 520 + 2 * 4 * TH

    wqkv_s = sb("wqkv_s", [128, DC * 1536], BF16, O_LG1)
    wstg_s = sb("wstg", [128, 2 * 3072], BF16, O_LG1)
    tabk_s = sb("tabk", [128, 3 * 128], F32, O_LG1 + 24576)
    tabq_s = sb("tabq", [128, 3 * 128], F32, O_LG1 + 26112)
    xbuf_s = sb("xbuf", [128, 3 * DM], F32, O_LG1 + 27648)
    ln1tm_s = sb("ln1tm", [128, 3 * DM], BF16, O_LG1 + 39936)
    ln1fm_s = sb("ln1fm", [128, 3 * DM], BF16, O_LG1 + 46080)
    tmpcc_s = sb("tmpcc", [128, 512], F32, O_LG1 + 52224)
    swe_s = sb("swe", [128, 256], F32, O_LG1 + 54272)
    swo_s = sb("swo", [128, 256], F32, O_LG1 + 55296)
    bn_s = sb("bn", [128, 48], F32, O_LG1 + 56320)
    st1_s = sb("st1", [128, 24], F32, O_LG1 + 56512)
    O_OFM = O_LG1 + 56608

    ofm_s = sb("ofm", [128, 8 * TH], BF16, O_OFM)
    O_D = O_OFM + 2 * 8 * TH

    pt_s = sb("pt", [128, 3 * 1024], BF16, O_D)
    stage_s = sb("stage", [128, 4 * 1024], BF16, O_D + 6144)
    den_s = sb("den", [128, 1024], F32, O_D + 14336)
    rs_s = sb("rs", [128, 1024], BF16, O_D + 18432)
    bn2_s = sb("bn2", [128, 32], F32, O_D + 20480)
    st2_s = sb("st2", [128, 16], F32, O_D + 20608)
    O_END_D = O_D + 20672

    overlay = (O_KVQ + 98304 + 16384 <= O_OFM
               and O_KVQ + 98304 + 16384 + 49152 <= O_END_D + 8192)
    if overlay:
        O_WG, O_WU = O_KVQ, O_KVQ + 49152
        O_WP = O_KVQ + 98304
        O_WD = O_WP + 16384
        O_FFNS = O_WD + 49152
    else:
        O_WG = O_KVQ
        O_WD = O_KVQ + 49152
        O_FFNS = O_WD + 49152
        O_WU = O_FFNS + 29952
        O_WP = O_WU + 49152
    wg_s = sb("wg_s", [128, DC * FH], BF16, O_WG)
    wu_s = sb("wu_s", [128, DC * FH], BF16, O_WU)
    wproj_s = sb("wproj_s", [128, DC * DM], BF16, O_WP)
    wd_s = sb("wd_s", [128, HC * DM], BF16, O_WD)
    ybuf_s = sb("ybuf", [128, 2 * DM], F32, O_WD)
    ln2f_s = sb("ln2f", [128, DM], F32, O_WD + 8192)
    ln2b_s = sb("ln2b", [128, 2 * DM], BF16, O_WD + 12288)
    xo_s = sb("xo", [128, DM], F32, O_WD + 16384)
    assert O_WD + 20480 <= O_OFM, (O_WD, O_OFM)
    ln2sl_s = sb("ln2sl", [128, 2 * DC * 256], BF16, O_FFNS)
    gs_s = sb("gs", [128, 2 * 512], BF16, O_FFNS + 8192)
    hfm_s = sb("hfm", [128, 3 * 512], BF16, O_FFNS + 10240)
    x2t_s = sb("x2t", [128, 2 * DM], F32, O_FFNS + 13312)
    yffn_s = sb("yffn", [128, 2 * DM], F32, O_FFNS + 21504)
    bn3_s = sb("bn3", [128, 32], F32, O_FFNS + 29696)
    st3_s = sb("st3", [128, 16], F32, O_FFNS + 29824)
    assert SB_BASE + max(O_END_D, O_FFNS + 29888) <= 229344, (O_END_D, O_FFNS)

    C = _Plan()
    SEMS = (["gld", "pe", "act", "dve", "dbg"]
            + [f"ldx{i}" for i in range(3)] + [f"ldt{i}" for i in range(3)]
            + [f"xo{i}" for i in range(2)] + [f"l2sl{i}" for i in range(2)]
            + [f"x2l{i}" for i in range(2)] + [f"sto{i}" for i in range(2)]
            + [f"x2st{i}" for i in range(2)] + [f"l2st{i}" for i in range(2)]
            + [f"outst{i}" for i in range(2)]
            + [f"wc{i}" for i in range(2)] + [f"ws{i}" for i in range(2)]
            + ["wub"])

    def ph_loads(C):
        if C.planning():
            for k in ["w_ident", "w_identf", "w_bqkv", "w_bproj", "w_mask",
                      "w_qkv0"]:
                C.ms("gp", "gld", (k,), k=16)
            C.ms("vector", "dve", ("memset",))
            return
        if C.on("gp"):
            C.ms("gp", "gld", ("w_ident",), C.E.dma_start(
                out=ident_s[:, :], in_=ident_d.ap()), k=16)
            C.ms("gp", "gld", ("w_identf",), C.E.dma_start(
                out=identf_s[:, :], in_=ident_d.ap()), k=16)
            C.ms("gp", "gld", ("w_bqkv",), C.E.dma_start(
                out=bqkv_s[:, :], in_=bqkv_d.ap()), k=16)
            C.ms("gp", "gld", ("w_bproj",), C.E.dma_start(
                out=bproj_s[:, :], in_=bproj_d.ap()), k=16)
            C.ms("gp", "gld", ("w_mask",), C.E.dma_start(
                out=maskb_s[:, :], in_=maskb_d.ap()), k=16)
            C.ms("gp", "gld", ("w_qkv0",), C.E.dma_start(
                out=wqkv_s[:, :].rearrange("p (c f) -> p c f", f=1536),
                in_=wqkv_d.ap()[:, 0:1536].rearrange("(c p) f -> p c f", p=128)),
                k=16)
        elif C.on("vector"):
            C.E.memset(ones_b[:, :], 1.0)
            C.E.memset(onesf_s[:, :], 1.0)
            C.E.memset(eps_s[:, 0:1], EPS)
            C.E.memset(eps_s[:, 1:2], EPS_FFN)
            C.E.memset(ofm_s[:, :], 0.0)
            i = C.E.memset(vtm_s[:, :], 1.0)
            C.ms("vector", "dve", ("memset",), i)

    def ph_qkv(C, g2, pq, ptpl, ptpq):
        wq_key = "w_qkv1" if g2 == 1 else "w_qkv0"
        if g2 == 1:
            if C.planning():
                C.ms("gp", "gld", ("w_qkv1",), k=16)
            elif C.on("gp"):
                C.E.wait_ge(C.sems["pe"], C.vals[("snap", "qkv0_end")]["pe"])
                C.E.wait_ge(C.sems["gld"], C.at("gld", "w_qkv0"))
                C.ms("gp", "gld", ("w_qkv1",), C.E.dma_start(
                    out=wqkv_s[:, :].rearrange("p (c f) -> p c f", f=1536),
                    in_=wqkv_d.ap()[:, 1536:3072].rearrange("(c p) f -> p c f",
                                                            p=128)), k=16)
            C.fence("tensor", "attn0_end", ["act", "dve"])
            C.fence("scalar", "attn0_end", ["pe", "dve"])
            C.fence("vector", "attn0_end", ["pe", "act"])
            C.fence("sync", "attn0_end", ["act", "dve", "pe"])

        def st_load(u):
            p = u % 3
            isq = u < NQT
            t = u if isq else u - NQT
            key = (g2, u)
            xb = xbuf_s[:, p * DM:(p + 1) * DM]
            if u >= 3:
                C.wt("sync", "act", "apply", g2, u - 3)
            if C.planning():
                C.ms("sync", f"ldx{u % 3}", ("x", *key), k=16)
            elif C.on("sync"):
                src = (xown_d if isq else x_d).ap()[t * 128:(t + 1) * 128, :]
                C.ms("sync", f"ldx{u % 3}", ("x", *key),
                     C.E.dma_start(out=xb, in_=src), k=16)

        def st_ldtab(u):
            p = u % 3
            isq = u < NQT
            t = u if isq else u - NQT
            key = (g2, u)
            if u >= 3:
                C.wt("sync", "dve", "rope", g2, u - 3)
            if C.planning():
                C.ms("sync", f"ldt{u % 3}", ("tb", *key), k=16)
            elif C.on("sync"):
                tb = (tabq_s if isq else tabk_s)[:, p * 128:(p + 1) * 128]
                src = (ropeq_d if isq else ropek_d).ap()[t * 128:(t + 1) * 128, :]
                C.ms("sync", f"ldt{u % 3}", ("tb", *key),
                     C.E.dma_start(out=tb, in_=src), k=16)

        def st_stats(u):
            p = u % 3
            key = (g2, u)
            xb = xbuf_s[:, p * DM:(p + 1) * DM]
            stm = st1_s[:, p * 8:p * 8 + 8]
            C.wt("vector", f"ldx{u % 3}", "x", *key)
            if u >= 3:
                C.wt("vector", "act", "apply", g2, u - 3)
            if C.planning():
                C.ms("vector", "dve", ("stats", *key))
            elif C.on("vector"):
                C.E.drain()
                C.E.bn_stats(bn_s[:, p * 16:p * 16 + 6], xb[:, 0:512])
                C.E.bn_stats(bn_s[:, p * 16 + 6:p * 16 + 12], xb[:, 512:1024])
                C.E.drain()
                i = C.E.bn_aggr(stm[:, 0:2], bn_s[:, p * 16:p * 16 + 12])
                C.ms("vector", "dve", ("stats", *key), i)

        def st_rstd(u):
            p = u % 3
            key = (g2, u)
            stm = st1_s[:, p * 8:p * 8 + 8]
            C.wt("scalar", "dve", "stats", *key)
            C.wt("scalar", "dve", "memset")
            if C.planning():
                C.ms("scalar", "act", ("rstd", *key))
            elif C.on("scalar"):
                C.E.activation(stm[:, 2:3], stm[:, 1:2], AF.Ln,
                               bias=eps_s[:, 0:1], scale=F1)
                C.E.drain()
                i = C.E.activation(stm[:, 3:4], stm[:, 2:3], AF.Exp,
                                   bias=F0, scale=FH2)
                C.ms("scalar", "act", ("rstd", *key), i)

        def st_nmr(u):
            p = u % 3
            key = (g2, u)
            stm = st1_s[:, p * 8:p * 8 + 8]
            C.wt("vector", "act", "rstd", *key)
            if C.planning():
                C.ms("vector", "dve", ("nmr", *key))
            elif C.on("vector"):
                i = C.E.tensor_scalar(stm[:, 4:5], stm[:, 0:1], stm[:, 3:4],
                                      -1.0, ALU.mult, ALU.mult)
                C.ms("vector", "dve", ("nmr", *key), i)

        def st_apply(u):
            p = u % 3
            key = (g2, u)
            stm = st1_s[:, p * 8:p * 8 + 8]
            xb = xbuf_s[:, p * DM:(p + 1) * DM]
            C.wt("scalar", "dve", "nmr", *key)
            if u >= 3:
                C.wt("scalar", "pe", "tpl", g2, u - 3)
            if C.planning():
                C.ms("scalar", "act", ("applyA", *key))
                C.ms("scalar", "act", ("apply", *key))
            elif C.on("scalar"):
                i = C.E.activation(ln1tm_s[:, p * DM:p * DM + 512],
                                   xb[:, 0:512], AF.Identity,
                                   bias=stm[:, 4:5], scale=stm[:, 3:4])
                C.ms("scalar", "act", ("applyA", *key), i)
                i = C.E.activation(ln1tm_s[:, p * DM + 512:(p + 1) * DM],
                                   xb[:, 512:1024], AF.Identity,
                                   bias=stm[:, 4:5], scale=stm[:, 3:4])
                C.ms("scalar", "act", ("apply", *key), i)

        def st_tpl(u):
            p = u % 3
            key = (g2, u)
            C.wt("tensor", "act", "applyA", *key)
            C.wt("tensor", "gld", "w_qkv0")
            if u >= 1:
                C.wt("tensor", "act", "evl", g2, u - 1)
            if C.planning():
                C.ms("tensor", "pe", ("tpl", *key))
            elif C.on("tensor"):
                for c in range(8):
                    if c == 4:
                        C.wt("tensor", "act", "apply", *key)
                    i = C.E.transpose(
                        ptpl[:, c * 128:(c + 1) * 128],
                        ln1tm_s[:, p * DM + c * 128:p * DM + (c + 1) * 128],
                        ident_s[:, :])
                C.ms("tensor", "pe", ("tpl", *key), i)

        def st_evl(u):
            p = u % 3
            key = (g2, u)
            C.wt("scalar", "pe", "tpl", *key)
            if u >= 3:
                C.wt("scalar", "pe", "tpq", g2, u - 3)
                C.wt("scalar", "dve", "rope", g2, u - 3)
            if C.planning():
                C.ms("scalar", "act", ("evlA", *key))
                C.ms("scalar", "act", ("evl", *key))
            elif C.on("scalar"):
                i = C.E.activation(ln1fm_s[:, p * DM:p * DM + 512],
                                   ptpl[:, 0:512], AF.Identity,
                                   bias=F0, scale=F1)
                C.ms("scalar", "act", ("evlA", *key), i)
                i = C.E.activation(ln1fm_s[:, p * DM + 512:(p + 1) * DM],
                                   ptpl[:, 512:1024], AF.Identity,
                                   bias=F0, scale=F1)
                C.ms("scalar", "act", ("evl", *key), i)

        def st_qkv(u):
            p = u % 3
            isq = u < NQT
            key = (g2, u)
            C.wt("tensor", "act", "evlA", *key)
            C.wt("tensor", "gld", wq_key)
            if u >= 3:
                C.wt("tensor", "dve", "rope", g2, u - 3)
                if u - 3 >= NQT:
                    C.wt("tensor", "act", "vev", g2, (u - 3) - NQT)
            if C.planning():
                C.ms("tensor", "pe", ("qkv", *key))
            elif C.on("tensor"):
                boff = 0 if isq else 512
                nb = 1 if isq else 2
                bph = g2 * 1536
                for b in range(nb):
                    C.E.matmul(pq[p][:, b * 512:(b + 1) * 512], ones_b[0:1, 0:128],
                               bqkv_s[0:1, bph + boff + b * 512:
                                      bph + boff + (b + 1) * 512],
                               start=True, stop=False, skip_group_check=True)
                for c in range(DC):
                    if c == 4:
                        C.wt("tensor", "act", "evl", *key)
                    lhsT = ln1fm_s[:, p * DM + c * 128:p * DM + (c + 1) * 128]
                    for b in range(nb):
                        i = C.E.matmul(
                            pq[p][:, b * 512:(b + 1) * 512], lhsT,
                            wqkv_s[:, c * 1536 + boff + b * 512:
                                   c * 1536 + boff + (b + 1) * 512],
                            start=False, stop=(c == DC - 1), skip_group_check=True)
                C.ms("tensor", "pe", ("qkv", *key), i)

        def st_rope(u):
            p = u % 3
            isq = u < NQT
            key = (g2, u)
            C.wt("vector", "pe", "qkv", *key)
            C.wt("vector", f"ldt{u % 3}", "tb", *key)
            if C.planning():
                C.ms("vector", "dve", ("rope", *key))
            elif C.on("vector"):
                qk = pq[p][:, 0:512]
                tb = (tabq_s if isq else tabk_s)[:, p * 128:(p + 1) * 128]
                r3 = lambda ap, s_: ap.rearrange("p (h s) -> p h s", s=s_)
                ev = qk.rearrange("p (h two s) -> p h two s", two=2, s=32)[:, :, 0, :]
                ov = qk.rearrange("p (h two s) -> p h two s", two=2, s=32)[:, :, 1, :]
                cc = tb[:, 0:64].unsqueeze(1).broadcast_to([128, 8, 64]) \
                    .rearrange("p h (two s) -> p h two s", two=2)
                sn = tb[:, 64:96].unsqueeze(1).broadcast_to([128, 8, 32])
                sp = tb[:, 96:128].unsqueeze(1).broadcast_to([128, 8, 32])
                C.E.drain()
                C.E.tensor_tensor(
                    tmpcc_s[:, 0:512].rearrange("p (h two s) -> p h two s",
                                                two=2, s=32),
                    qk.rearrange("p (h two s) -> p h two s", two=2, s=32),
                    cc, ALU.mult)
                C.E.tensor_tensor(r3(swe_s[:, 0:256], 32), ov, sn, ALU.mult)
                C.E.tensor_tensor(r3(swo_s[:, 0:256], 32), ev, sp, ALU.mult)
                C.E.drain()
                tme = tmpcc_s[:, 0:512].rearrange(
                    "p (h two s) -> p h two s", two=2, s=32)[:, :, 0, :]
                tmo = tmpcc_s[:, 0:512].rearrange(
                    "p (h two s) -> p h two s", two=2, s=32)[:, :, 1, :]
                dst = ln1fm_s[:, p * DM:p * DM + 512].rearrange(
                    "p (h two s) -> p h two s", two=2, s=32)
                C.E.tensor_tensor(dst[:, :, 0, :], tme, r3(swe_s[:, 0:256], 32),
                                  ALU.add)
                i = C.E.tensor_tensor(dst[:, :, 1, :], tmo, r3(swo_s[:, 0:256], 32),
                                      ALU.add)
                C.ms("vector", "dve", ("rope", *key), i)

        def st_vev(u):
            p = u % 3
            t = u - NQT
            key = (g2, u)
            C.wt("scalar", "pe", "qkv", *key)
            if C.planning():
                C.ms("scalar", "act", ("vev", g2, t))
            elif C.on("scalar"):
                dst = vtm_s[:, t * 520:(t + 1) * 520].rearrange(
                    "p (h s) -> p h s", s=65)[:, :, 0:64]
                i = C.E.activation(
                    dst, pq[p][:, 512:1024].rearrange("p (h s) -> p h s", s=64),
                    AF.Identity, bias=F0, scale=F1)
                C.ms("scalar", "act", ("vev", g2, t), i)

        def st_tpq(u):
            p = u % 3
            key = (g2, u)
            C.wt("tensor", "dve", "rope", *key)
            if u >= 1:
                C.wt("tensor", "dve", "evq", g2, u - 1)
            if C.planning():
                C.ms("tensor", "pe", ("tpq", *key))
            elif C.on("tensor"):
                src = ln1fm_s[:, p * DM:p * DM + 512]
                for c in range(4):
                    i = C.E.transpose(ptpq[:, c * 128:(c + 1) * 128],
                                      src[:, c * 128:(c + 1) * 128], ident_s[:, :])
                C.ms("tensor", "pe", ("tpq", *key), i)

        def st_evq(u):
            isq = u < NQT
            t = u if isq else u - NQT
            key = (g2, u)
            C.wt("vector", "pe", "tpq", *key)
            if C.planning():
                C.ms("vector", "dve", ("evq", *key))
            elif C.on("vector"):
                if isq:
                    dst = qfm_s[:, :].rearrange("p (h n f) -> p h n f", h=4, n=NQT)
                else:
                    dst = kfm_s[:, :].rearrange("p (h n f) -> p h n f", h=4, n=NT)
                i = C.E.tensor_copy(dst[:, :, t, :],
                                    ptpq[:, 0:512].rearrange("p (h f) -> p h f", h=4))
                C.ms("vector", "dve", ("evq", *key), i)

        for v in range(-1, NU + 3):
            if v + 1 < NU:
                st_load(v + 1)
                st_stats(v + 1)
                st_rstd(v + 1)
                st_nmr(v + 1)
            if 0 <= v < NU:
                st_ldtab(v)
                st_apply(v)
            if 0 <= v - 1 < NU:
                st_tpl(v - 1)
                st_evl(v - 1)
            if 0 <= v - 2 < NU:
                st_qkv(v - 2)
                st_rope(v - 2)
                if v - 2 >= NQT:
                    st_vev(v - 2)
            if 0 <= v - 3 < NU:
                st_tpq(v - 3)
                st_evq(v - 3)
        C.snap(f"qkv{g2}_end")

    def ph_attn(C, g2, pS, pO, pp=None, with_proj=False):
        NPS = 2 if with_proj else 3
        C.fence("tensor", f"qkv{g2}_end", ["dve", "act"])
        C.fence("scalar", f"qkv{g2}_end", ["pe", "dve"])
        C.fence("vector", f"qkv{g2}_end", ["pe", "act", "gld"])
        if C.planning():
            C.ms("vector", "dve", ("amemset", g2))
        elif C.on("vector"):
            C.E.drain()
            for i0 in range(NPS):
                C.E.memset(pS[i0][:, :], 0.0)
            i = C.E.memset(pO[:, :], 0.0)
            C.ms("vector", "dve", ("amemset", g2), i)
        if g2 == 1:
            if C.planning():
                C.ms("gp", "gld", ("w_proj",), k=16)
            elif C.on("gp"):
                snapshot = C.vals[("snap", "qkv1_end")]
                for s in ["pe", "dve", "act"]:
                    C.E.wait_ge(C.sems[s], snapshot[s])
                C.E.wait_ge(C.sems["gld"], C.at("gld", "w_qkv1"))
                C.ms("gp", "gld", ("w_proj",), C.E.dma_start(
                    out=wproj_s[:, :].rearrange("p (c f) -> p c f", f=DM),
                    in_=wproj_d.ap().rearrange("(c p) f -> p c f", p=128)), k=16)

        def em_bounce(i):
            q = i % 2
            w, j = divmod(i, 8)
            if i >= 2:
                C.wt("gp", f"ws{q}", "wstr", i - 2)
            if C.planning():
                C.ms("gp", f"wc{q}", ("wcst", i), k=16)
            elif C.on("gp"):
                stg = wstg_s[:, q * 3072:(q + 1) * 3072]
                if w < 2:
                    src_d = wg_d if w == 0 else wu_d
                    dma = C.E.dma_start(
                        out=stg.rearrange("p (c f) -> p c f", f=FH),
                        in_=src_d.ap().rearrange("(c p) f -> p c f", p=128)
                        [:, j:j + 1, :])
                else:
                    dma = C.E.dma_start(
                        out=stg.rearrange("p (c f) -> p c f", f=DM),
                        in_=wd_d.ap().rearrange("(c p) f -> p c f", p=128)
                        [:, 3 * j:3 * j + 3, :])
                C.ms("gp", f"wc{q}", ("wcst", i), dma, k=16)
            C.wt("gp", f"wc{q}", "wcst", i)
            if C.planning():
                C.ms("gp", f"ws{q}", ("wstr", i), k=16)
            elif C.on("gp"):
                C.ms("gp", f"ws{q}", ("wstr", i), C.E.dma_start(
                    out=wcat_d.ap()[:, i * 3072:(i + 1) * 3072],
                    in_=wstg_s[:, q * 3072:(q + 1) * 3072]), k=16)

        hist = []

        def ps_wait(eng):
            if len(hist) >= NPS:
                sem, key = hist[-NPS]
                C.wt(eng, sem, *key)
            else:
                C.wt(eng, "dve", "amemset", g2)
            return len(hist) % NPS

        def take_ps(eng, reader):
            ri = ps_wait(eng)
            hist.append(reader)
            return ri

        proj = mk_proj(C, pp, pS, take_ps) if with_proj else None

        pt_hist = []

        def pt_wait(eng):
            if len(pt_hist) >= 3:
                C.wt(eng, "pe", *pt_hist[-3])
            return len(pt_hist) % 3

        pt_slot = {}

        def em_S(s, hp, kb):
            key = (g2, s, hp, kb)
            ri = ps_wait("tensor")
            hist.append(("act", ("exp",) + key))
            if C.planning():
                C.ms("tensor", "pe", ("s", *key))
            elif C.on("tensor"):
                kA = kfm_s[0:64, hp * T + kb * 128:hp * T + (kb + 1) * 128]
                kB = kfm_s[64:128, hp * T + kb * 128:hp * T + (kb + 1) * 128]
                qA = qfm_s[0:64, hp * TH + s * 512:hp * TH + (s + 1) * 512]
                qB = qfm_s[64:128, hp * TH + s * 512:hp * TH + (s + 1) * 512]
                C.E.matmul(pS[ri][:, 0:512], kA, qA, start=True, stop=True,
                           tile_position=(0, 0))
                i = C.E.matmul(pS[ri][:, 512:1024], kB, qB,
                               start=True, stop=True, tile_position=(64, 0))
                C.ms("tensor", "pe", ("s", *key), i)
            return ri

        def em_mask(s, hp, kb, nkb):
            jj = kb - (nkb - 8)
            key = (g2, s, hp, kb)
            pi = pt_slot[key]
            C.wt("vector", "act", "exp", *key)
            C.wt("vector", "gld", "w_qkv0")
            if C.planning():
                C.ms("vector", "dve", ("mask", *key))
            elif C.on("vector"):
                mk = maskb_s[:, jj * 512:(jj + 1) * 512] \
                    .unsqueeze(1).broadcast_to([128, 2, 512])
                pv = pt_s[:, pi * 1024:(pi + 1) * 1024].rearrange(
                    "p (two f) -> p two f", two=2)
                i = C.E.tensor_tensor(pv, pv, mk, ALU.mult)
                C.ms("vector", "dve", ("mask", *key), i)

        def em_exp(s, hp, kb, ri, masked):
            key = (g2, s, hp, kb)
            C.wt("scalar", "pe", "s", *key)
            pi = pt_wait("scalar")
            pt_hist.append(("av",) + key)
            pt_slot[key] = pi
            if C.planning():
                C.ms("scalar", "act", ("exp", *key))
            elif C.on("scalar"):
                i = C.E.activation(pt_s[:, pi * 1024:(pi + 1) * 1024],
                                   pS[ri][:, 0:1024], AF.Exp,
                                   bias=F0, scale=np.float32(0.125))
                C.ms("scalar", "act", ("exp", *key), i)

        def em_AV(s, hp, kb, nkb):
            key = (g2, s, hp, kb)
            pi = pt_slot.pop(key)
            masked = kb >= nkb - 8
            if masked:
                C.wt("tensor", "dve", "mask", *key)
            else:
                C.wt("tensor", "act", "exp", *key)
            if kb == 0:
                prev = (g2, s, hp - 1) if hp > 0 else (
                    (g2, s - 1, 3) if s > 0 else
                    ((0, NSLOT - 1, 3) if g2 > 0 else None))
                if prev is not None:
                    C.wt("tensor", "act", "ocopy", *prev)
            if C.planning():
                C.ms("tensor", "pe", ("av", *key))
            elif C.on("tensor"):
                vA = vtm_s[:, kb * 520 + 2 * hp * 65:kb * 520 + 2 * hp * 65 + 65]
                vB = vtm_s[:, kb * 520 + (2 * hp + 1) * 65:
                           kb * 520 + (2 * hp + 1) * 65 + 65]
                C.E.matmul(pO[0:65, 0:512], vA,
                           pt_s[:, pi * 1024:pi * 1024 + 512],
                           start=(kb == 0), stop=(kb == nkb - 1),
                           skip_group_check=True)
                i = C.E.matmul(pO[0:65, 512:1024], vB,
                               pt_s[:, pi * 1024 + 512:(pi + 1) * 1024],
                               start=(kb == 0), stop=(kb == nkb - 1),
                               skip_group_check=True)
                C.ms("tensor", "pe", ("av", *key), i)

        def em_epi_a(s, hp, nkb):
            prev = (g2, s, hp - 1) if hp > 0 else (
                (g2, s - 1, 3) if s > 0 else
                ((0, NSLOT - 1, 3) if g2 > 0 else None))
            C.wt("scalar", "pe", "av", g2, s, hp, nkb - 1)
            if prev is not None:
                C.wt("scalar", "dve", "norm", *prev)
            if C.planning():
                C.ms("scalar", "act", ("ocopy", g2, s, hp))
            elif C.on("scalar"):
                i = C.E.activation(den_s[0:65, 0:1024], pO[0:65, 0:1024],
                                   AF.Identity, bias=F0, scale=F1)
                C.ms("scalar", "act", ("ocopy", g2, s, hp), i)

        def em_epi_b1(s, hp):
            if C.planning():
                C.ms("scalar", "act", ("rcp2", g2, s, hp))
            elif C.on("scalar"):
                C.E.drain()
                C.E.activation(den_s[64:65, 0:1024], den_s[64:65, 0:1024],
                               AF.Ln, bias=F0, scale=F1)
                C.E.drain()
                i = C.E.activation(rs_s[64:65, 0:1024], den_s[64:65, 0:1024],
                                   AF.Exp, bias=F0, scale=np.float32(-1.0))
                C.ms("scalar", "act", ("rcp2", g2, s, hp), i)

        def em_epi_b(s, hp):
            C.wt("tensor", "act", "rcp2", g2, s, hp)
            ri = ps_wait("tensor")
            hist.append(("dve", ("norm", g2, s, hp)))
            if C.planning():
                C.ms("tensor", "pe", ("bcast", g2, s, hp))
            elif C.on("tensor"):
                C.E.matmul(pS[ri][0:64, 0:512], ones_b[64:65, 0:64],
                           rs_s[64:65, 0:512], start=True, stop=True,
                           tile_position=(64, 0))
                i = C.E.matmul(pS[ri][0:64, 512:1024], ones_b[64:65, 0:64],
                               rs_s[64:65, 512:1024], start=True, stop=True,
                               tile_position=(64, 0))
                C.ms("tensor", "pe", ("bcast", g2, s, hp), i)
            C.wt("vector", "pe", "bcast", g2, s, hp)
            if hp == 0:
                if s > 0:
                    C.wt("vector", f"sto{(s - 1) % 2}", "ostore", g2, s - 1)
                elif g2 > 0:
                    C.wt("vector", f"sto{(NSLOT - 1) % 2}", "ostore",
                         0, NSLOT - 1)
            if C.planning():
                C.ms("vector", "dve", ("norm", g2, s, hp))
            elif C.on("vector"):
                C.E.drain()
                dst = stage_s[0:64, hp * 1024:(hp + 1) * 1024]
                i = C.E.tensor_tensor(dst, den_s[0:64, 0:1024],
                                      pS[ri][0:64, 0:1024], ALU.mult)
                C.ms("vector", "dve", ("norm", g2, s, hp), i)

        if g2 == 1 and stage >= 5:
            for i2 in range(24):
                em_bounce(i2)
        pending_epi = None
        for s in range(NSLOT):
            nkb = csched(s)
            for hp in range(4):
                for j in range(nkb + 2):
                    if j < nkb:
                        masked = j >= nkb - 8
                        ri = em_S(s, hp, j)
                        em_exp(s, hp, j, ri, masked)
                        if masked:
                            em_mask(s, hp, j, nkb)
                    if j == 2 and pending_epi is not None:
                        em_epi_b1(*pending_epi)
                    if j == 6 and pending_epi is not None:
                        em_epi_b(*pending_epi)
                        pending_epi = None
                    if (j in (2, 4, 6, 8, 10) and proj is not None
                            and s >= 1):
                        proj(max_tile=4 * s - 1)
                    if j >= 2:
                        em_AV(s, hp, j - 2, nkb)
                em_epi_a(s, hp, nkb)
                pending_epi = (s, hp)
            if pending_epi is not None and pending_epi[0] == s:
                em_epi_b1(*pending_epi)
                em_epi_b(*pending_epi)
                pending_epi = None
            for hp in range(4):
                C.wt("sync", "dve", "norm", g2, s, hp)
            if C.planning():
                C.ms("sync", f"sto{s % 2}", ("ostore", g2, s), k=32)
            elif C.on("sync"):
                ofv = ofm_s[:, :].rearrange("p (c f) -> p c f", c=8)
                stv = stage_s[0:64, :].rearrange("p (h two f) -> p h two f",
                                                 two=2, f=512)
                dstA = ofv[0:64, 4 * g2:4 * g2 + 4, s * 512:(s + 1) * 512]
                dstB = ofv[64:128, 4 * g2:4 * g2 + 4, s * 512:(s + 1) * 512]
                C.E.dma_start(out=dstA, in_=stv[:, :, 0, :]).then_inc(
                    C.sems[f"sto{s % 2}"], 16)
                i = C.E.dma_start(out=dstB, in_=stv[:, :, 1, :])
                C.ms("sync", f"sto{s % 2}", ("ostore", g2, s), i, k=16)
        if g2 == 1 and stage >= 5:
            C.snap("attn1_slots")
            if C.planning():
                C.ms("gp", "gld", ("w_g",), k=16)
            elif C.on("gp"):
                snapshot = C.vals[("snap", "attn1_slots")]
                for s_ in ["pe", "dve", "act"]:
                    C.E.wait_ge(C.sems[s_], snapshot[s_])
                wlast = (6, 7) if overlay else (22, 23)
                C.E.wait_ge(C.sems["ws0"], C.at("ws0", "wstr", wlast[0]))
                C.E.wait_ge(C.sems["ws1"], C.at("ws1", "wstr", wlast[1]))
                C.E.wait_ge(C.sems["gld"], C.at("gld", "w_proj"))
                C.ms("gp", "gld", ("w_g",), C.E.dma_start(
                    out=wg_s[:, :], in_=wcat_d.ap()[:, 0:24576]), k=16)
            if C.planning():
                C.ms("sync", "wub", ("w_u2",), k=16)
            elif C.on("sync"):
                snapshot = C.vals[("snap", "attn1_slots")]
                for s_ in ["pe", "dve", "act"]:
                    C.E.wait_ge(C.sems[s_], snapshot[s_])
                C.E.wait_ge(C.sems["ws0"], C.at("ws0", "wstr", 22))
                C.E.wait_ge(C.sems["ws1"], C.at("ws1", "wstr", 23))
                C.ms("sync", "wub", ("w_u2",), C.E.dma_start(
                    out=wu_s[:, :], in_=wcat_d.ap()[:, 24576:49152]), k=16)
        if proj is not None:
            for _ in range(3 * NQT):
                proj()
        if g2 == 1 and stage >= 5:
            C.snap("attn1_tail")
            if C.planning():
                C.ms("gp", "gld", ("w_d",), k=16)
            elif C.on("gp"):
                snapshot = C.vals[("snap", "attn1_tail")]
                for s_ in ["pe", "dve", "act", "x2st0", "x2st1",
                           "xo0", "xo1", "l2st0", "l2st1"]:
                    if snapshot.get(s_, 0) > 0:
                        C.E.wait_ge(C.sems[s_], snapshot[s_])
                C.E.wait_ge(C.sems["gld"], C.at("gld", "w_g"))
                C.E.wait_ge(C.sems["ws0"], C.at("ws0", "wstr", 22))
                C.E.wait_ge(C.sems["ws1"], C.at("ws1", "wstr", 23))
                C.ms("gp", "gld", ("w_d",), C.E.dma_start(
                    out=wd_s[:, :], in_=wcat_d.ap()[:, 49152:73728]), k=16)
        C.snap(f"attn{g2}_end")

    def mk_proj(C, pp, pS, take_ps):
        state = {"pos": 0}

        def em_front_a(t):
            p = t % 2
            if t >= 1:
                C.wt("sync", "dve", "x2", t - 1)
            if C.planning():
                C.ms("sync", f"xo{p}", ("xo", t), k=16)
            elif C.on("sync"):
                C.ms("sync", f"xo{p}", ("xo", t), C.E.dma_start(
                    out=xo_s[:, 0:DM],
                    in_=xown_d.ap()[t * 128:(t + 1) * 128, :]), k=16)

        def em_front(t):
            p = t % 2
            C.wt("tensor", "gld", "w_proj")
            sreq = t // 4
            C.wt("tensor", f"sto{sreq % 2}", "ostore", 1, sreq)
            if t >= 1:
                C.wt("tensor", "dve", "papply", t - 1)
            if C.planning():
                C.ms("tensor", "pe", ("proj", t))
            elif C.on("tensor"):
                for b in range(2):
                    C.E.matmul(pp[0][:, b * 512:(b + 1) * 512], ones_b[0:1, 0:128],
                               bproj_s[0:1, b * 512:(b + 1) * 512],
                               start=True, stop=False, skip_group_check=True)
                for c in range(DC):
                    lhsT = ofm_s[:, c * TH + t * 128:c * TH + (t + 1) * 128]
                    for b in range(2):
                        i = C.E.matmul(
                            pp[0][:, b * 512:(b + 1) * 512], lhsT,
                            wproj_s[:, c * DM + b * 512:c * DM + (b + 1) * 512],
                            start=False, stop=(c == DC - 1), skip_group_check=True)
                C.ms("tensor", "pe", ("proj", t), i)
            C.wt("vector", "pe", "proj", t)
            if t >= 2:
                C.wt("vector", "act", "l2rstd", t - 2)
            if C.planning():
                C.ms("vector", "dve", ("pstats", t))
            elif C.on("vector"):
                C.E.drain()
                C.E.bn_stats(bn2_s[:, p * 16:p * 16 + 6], pp[0][:, 0:512])
                C.E.bn_stats(bn2_s[:, p * 16 + 6:p * 16 + 12], pp[0][:, 512:1024])
                C.E.drain()
                i = C.E.bn_aggr(st2_s[:, p * 8:p * 8 + 2],
                                bn2_s[:, p * 16:p * 16 + 12])
                C.ms("vector", "dve", ("pstats", t), i)
            C.wt("scalar", "dve", "pstats", t)
            if C.planning():
                C.ms("scalar", "act", ("prstd", t))
            elif C.on("scalar"):
                C.E.activation(st2_s[:, p * 8 + 2:p * 8 + 3],
                               st2_s[:, p * 8 + 1:p * 8 + 2], AF.Ln,
                               bias=eps_s[:, 0:1], scale=F1)
                C.E.drain()
                i = C.E.activation(st2_s[:, p * 8 + 3:p * 8 + 4],
                                   st2_s[:, p * 8 + 2:p * 8 + 3], AF.Exp,
                                   bias=F0, scale=FH2)
                C.ms("scalar", "act", ("prstd", t), i)
            C.wt("vector", "act", "prstd", t)
            C.wt("vector", f"xo{p}", "xo", t)
            if t >= 2:
                C.wt("vector", f"x2st{p}", "x2st", t - 2)
            if C.planning():
                C.ms("vector", "dve", ("papply", t))
                C.ms("vector", "dve", ("x2", t))
            elif C.on("vector"):
                yb = ybuf_s[:, p * DM:(p + 1) * DM]
                i = C.E.tensor_scalar(yb, pp[0][:, 0:1024],
                                      st2_s[:, p * 8:p * 8 + 1],
                                      st2_s[:, p * 8 + 3:p * 8 + 4],
                                      ALU.subtract, ALU.mult)
                C.ms("vector", "dve", ("papply", t), i)
                C.E.drain()
                i = C.E.tensor_tensor(yb, yb, xo_s[:, 0:DM], ALU.add)
                C.ms("vector", "dve", ("x2", t), i)
            C.wt("sync", "dve", "x2", t)
            if C.planning():
                C.ms("sync", f"x2st{p}", ("x2st", t), k=16)
            elif C.on("sync"):
                C.ms("sync", f"x2st{p}", ("x2st", t), C.E.dma_start(
                    out=x2_d.ap()[t * 128:(t + 1) * 128, :],
                    in_=ybuf_s[:, p * DM:(p + 1) * DM]), k=16)

        def em_B(t):
            p = t % 2
            if C.planning():
                C.ms("vector", "dve", ("l2stats", t))
            elif C.on("vector"):
                yb = ybuf_s[:, p * DM:(p + 1) * DM]
                C.E.drain()
                C.E.bn_stats(bn2_s[:, p * 16:p * 16 + 6], yb[:, 0:512])
                C.E.bn_stats(bn2_s[:, p * 16 + 6:p * 16 + 12], yb[:, 512:1024])
                C.E.drain()
                i = C.E.bn_aggr(st2_s[:, p * 8:p * 8 + 2],
                                bn2_s[:, p * 16:p * 16 + 12])
                C.ms("vector", "dve", ("l2stats", t), i)
            C.wt("scalar", "dve", "l2stats", t)
            if C.planning():
                C.ms("scalar", "act", ("l2rstd", t))
            elif C.on("scalar"):
                C.E.activation(st2_s[:, p * 8 + 2:p * 8 + 3],
                               st2_s[:, p * 8 + 1:p * 8 + 2], AF.Ln,
                               bias=eps_s[:, 0:1], scale=F1)
                C.E.drain()
                i = C.E.activation(st2_s[:, p * 8 + 3:p * 8 + 4],
                                   st2_s[:, p * 8 + 2:p * 8 + 3], AF.Exp,
                                   bias=F0, scale=FH2)
                C.ms("scalar", "act", ("l2rstd", t), i)

        def em_back_a(t):
            p = t % 2
            C.wt("vector", "act", "l2rstd", t)
            if t >= 1:
                C.wt("vector", "pe", "tp2", t - 1)
            if C.planning():
                C.ms("vector", "dve", ("l2apply", t))
            elif C.on("vector"):
                C.E.drain()
                i = C.E.tensor_scalar(ln2f_s[:, 0:DM],
                                      ybuf_s[:, p * DM:(p + 1) * DM],
                                      st2_s[:, p * 8:p * 8 + 1],
                                      st2_s[:, p * 8 + 3:p * 8 + 4],
                                      ALU.subtract, ALU.mult)
                C.ms("vector", "dve", ("l2apply", t), i)

        def em_back_b(t):
            p = t % 2
            C.wt("tensor", "dve", "l2apply", t)
            C.wt("tensor", "gld", "w_qkv0")
            ri2 = take_ps("tensor", ("dve", ("l2ev", t)))
            if C.planning():
                C.ms("tensor", "pe", ("tp2", t))
            elif C.on("tensor"):
                for c in range(8):
                    i = C.E.transpose(
                        pS[ri2][:, c * 128:(c + 1) * 128],
                        ln2f_s[:, c * 128:(c + 1) * 128],
                        identf_s[:, :])
                C.ms("tensor", "pe", ("tp2", t), i)
            C.wt("vector", "pe", "tp2", t)
            if t >= 2:
                C.wt("vector", f"l2st{p}", "l2st", t - 2)
            if C.planning():
                C.ms("vector", "dve", ("l2ev", t))
            elif C.on("vector"):
                C.E.drain()
                i = C.E.tensor_copy(ln2b_s[:, p * DM:(p + 1) * DM],
                                    pS[ri2][:, :])
                C.ms("vector", "dve", ("l2ev", t), i)
            C.wt("sync", "dve", "l2ev", t)
            if C.planning():
                C.ms("sync", f"l2st{p}", ("l2st", t), k=16)
            elif C.on("sync"):
                dst = ln2_d.ap().rearrange("(c p) t -> p c t", p=128)[
                    :, :, t * 128:(t + 1) * 128]
                C.ms("sync", f"l2st{p}", ("l2st", t), C.E.dma_start(
                    out=dst, in_=ln2b_s[:, p * DM:(p + 1) * DM].rearrange(
                        "p (c t) -> p c t", c=8)), k=16)

        def step(max_tile=None):
            i = state["pos"]
            t, k = divmod(i, 3)
            if t >= NQT:
                return
            if k == 0 and max_tile is not None and t > max_tile:
                return
            state["pos"] += 1
            if i == 0:
                em_front_a(0)
            if k == 0:
                em_front(t)
            elif k == 1:
                if t + 1 < NQT:
                    em_front_a(t + 1)
                em_B(t)
            else:
                em_back_a(t)
                em_back_b(t)
        return step

    def ph_ffn(C, pgu, pdn):
        C.fence("tensor", "attn1_end", ["dve", "act"])
        C.fence("scalar", "attn1_end", ["pe", "dve"])
        C.fence("vector", "attn1_end", ["pe", "act"])
        C.fence("sync", "attn1_end", ["pe", "dve", "sto0", "sto1",
                                      "x2st0", "x2st1", "l2st0", "l2st1"])
        def em_l2sl(sl):
            p = sl % 2
            if sl >= 2:
                C.wt("sync", "pe", "gu", sl - 2, HC // 2 - 1)
            if C.planning():
                C.ms("sync", f"l2sl{p}", ("l2sl", sl), k=16)
            elif C.on("sync"):
                src = ln2_d.ap().rearrange("(c p) t -> p c t", p=128)[
                    :, :, sl * 256:(sl + 1) * 256]
                C.ms("sync", f"l2sl{p}", ("l2sl", sl), C.E.dma_start(
                    out=ln2sl_s[:, p * 2048:(p + 1) * 2048].rearrange(
                        "p (c t) -> p c t", c=8), in_=src), k=16)

        def em_x2l(sl2):
            for tt2 in range(2):
                t = sl2 * 2 + tt2
                if sl2 >= 1:
                    C.wt("sync", "dve", "fout", sl2 - 1, tt2)
                if C.planning():
                    C.ms("sync", f"x2l{tt2}", ("x2l", sl2, tt2), k=16)
                elif C.on("sync"):
                    C.ms("sync", f"x2l{tt2}", ("x2l", sl2, tt2), C.E.dma_start(
                        out=x2t_s[:, tt2 * DM:(tt2 + 1) * DM],
                        in_=x2_d.ap()[t * 128:(t + 1) * 128, :]), k=16)

        def em_chain_a(sl2):
            for tt2 in range(2):
                C.wt("vector", "pe", "dn", sl2, HC // 2 - 1)
                if C.planning():
                    C.ms("vector", "dve", ("fstats", sl2, tt2))
                elif C.on("vector"):
                    C.E.drain()
                    C.E.bn_stats(bn3_s[:, tt2 * 16:tt2 * 16 + 6],
                                 pdn[tt2][:, 0:512])
                    C.E.bn_stats(bn3_s[:, tt2 * 16 + 6:tt2 * 16 + 12],
                                 pdn[tt2][:, 512:1024])
                    C.E.drain()
                    C.E.bn_aggr(st3_s[:, tt2 * 8:tt2 * 8 + 2],
                                bn3_s[:, tt2 * 16:tt2 * 16 + 12])
                    C.E.drain()
                    C.E.tensor_scalar(st3_s[:, tt2 * 8 + 2:tt2 * 8 + 3],
                                      st3_s[:, tt2 * 8 + 1:tt2 * 8 + 2],
                                      EPS_FFN, 0.0, ALU.add, ALU.bypass)
                    C.E.drain()
                    i = C.E.reciprocal(st3_s[:, tt2 * 8 + 4:tt2 * 8 + 5],
                                       st3_s[:, tt2 * 8 + 2:tt2 * 8 + 3])
                    C.ms("vector", "dve", ("fstats", sl2, tt2), i)
                C.wt("scalar", "dve", "fstats", sl2, tt2)
                if C.planning():
                    C.ms("scalar", "act", ("frstd", sl2, tt2))
                elif C.on("scalar"):
                    i = C.E.activation(st3_s[:, tt2 * 8 + 3:tt2 * 8 + 4],
                                       st3_s[:, tt2 * 8 + 4:tt2 * 8 + 5],
                                       AF.Sqrt, bias=F0, scale=F1)
                    C.ms("scalar", "act", ("frstd", sl2, tt2), i)

        def em_chain_b(sl2):
            for tt2 in range(2):
                C.wt("vector", "act", "frstd", sl2, tt2)
                C.wt("vector", f"x2l{tt2}", "x2l", sl2, tt2)
                if sl2 >= 1:
                    C.wt("vector", f"outst{tt2}", "outst", sl2 - 1, tt2)
                if C.planning():
                    C.ms("vector", "dve", ("fapply", sl2, tt2))
                    C.ms("vector", "dve", ("fout", sl2, tt2))
                elif C.on("vector"):
                    C.E.drain()
                    yf = yffn_s[:, tt2 * DM:(tt2 + 1) * DM]
                    i = C.E.tensor_scalar(yf, pdn[tt2][:, 0:1024],
                                          st3_s[:, tt2 * 8:tt2 * 8 + 1],
                                          st3_s[:, tt2 * 8 + 3:tt2 * 8 + 4],
                                          ALU.subtract, ALU.mult)
                    C.ms("vector", "dve", ("fapply", sl2, tt2), i)
                    C.E.drain()
                    i = C.E.tensor_tensor(yf, yf,
                                          x2t_s[:, tt2 * DM:(tt2 + 1) * DM],
                                          ALU.add)
                    C.ms("vector", "dve", ("fout", sl2, tt2), i)
                C.wt("sync", "dve", "fout", sl2, tt2)
                if C.planning():
                    C.ms("sync", f"outst{tt2}", ("outst", sl2, tt2), k=16)
                elif C.on("sync"):
                    t = sl2 * 2 + tt2
                    C.ms("sync", f"outst{tt2}", ("outst", sl2, tt2),
                         C.E.dma_start(
                             out=out_d.ap()[t * 128:(t + 1) * 128, :],
                             in_=yffn_s[:, tt2 * DM:(tt2 + 1) * DM]), k=16)

        em_l2sl(0)
        em_x2l(0)
        for sl in range(NSL + 1):
            p = sl % 2
            def em_gu(hcp):
                pg = hcp % 2
                C.wt("tensor", f"l2sl{p}", "l2sl", sl)
                C.wt("tensor", "gld", "w_g")
                if hcp >= 2:
                    C.wt("tensor", "dve", "hmul", sl, hcp - 2)
                elif sl > 0:
                    C.wt("tensor", "dve", "hmul", sl - 1, HC // 2 - 2 + hcp)
                if C.planning():
                    C.ms("tensor", "pe", ("gu", sl, hcp))
                elif C.on("tensor"):
                    for j in range(2):
                        hc = 2 * hcp + j
                        for c in range(DC):
                            C.E.matmul(
                                pgu[pg][:, j * 256:(j + 1) * 256],
                                wg_s[:, c * FH + hc * 128:c * FH + (hc + 1) * 128],
                                ln2sl_s[:, p * 2048 + c * 256:
                                        p * 2048 + (c + 1) * 256],
                                start=(c == 0), stop=(c == DC - 1),
                                skip_group_check=True)
                    C.wt("tensor", "wub", "w_u2")
                    for j in range(2):
                        hc = 2 * hcp + j
                        for c in range(DC):
                            i = C.E.matmul(
                                pgu[pg][:, 512 + j * 256:512 + (j + 1) * 256],
                                wu_s[:, c * FH + hc * 128:c * FH + (hc + 1) * 128],
                                ln2sl_s[:, p * 2048 + c * 256:
                                        p * 2048 + (c + 1) * 256],
                                start=(c == 0), stop=(c == DC - 1),
                                skip_group_check=True)
                    C.ms("tensor", "pe", ("gu", sl, hcp), i)
                C.wt("scalar", "pe", "gu", sl, hcp)
                if hcp >= 2:
                    C.wt("scalar", "dve", "hmul", sl, hcp - 2)
                elif sl > 0:
                    C.wt("scalar", "dve", "hmul", sl - 1, HC // 2 - 2 + hcp)
                if C.planning():
                    C.ms("scalar", "act", ("silu", sl, hcp))
                elif C.on("scalar"):
                    i = C.E.activation(gs_s[:, pg * 512:(pg + 1) * 512],
                                       pgu[pg][:, 0:512], AF.Silu,
                                       bias=F0, scale=np.float32(C127))
                    C.ms("scalar", "act", ("silu", sl, hcp), i)
                C.wt("vector", "act", "silu", sl, hcp)
                if hcp >= 3:
                    C.wt("vector", "pe", "dn", sl, hcp - 3)
                elif sl > 0:
                    C.wt("vector", "pe", "dn", sl - 1, HC // 2 - 3 + hcp)
                if C.planning():
                    C.ms("vector", "dve", ("hmul", sl, hcp))
                elif C.on("vector"):
                    p3 = hcp % 3
                    i = C.E.tensor_tensor(hfm_s[:, p3 * 512:(p3 + 1) * 512],
                                          gs_s[:, pg * 512:(pg + 1) * 512],
                                          pgu[pg][:, 512:1024], ALU.mult)
                    C.ms("vector", "dve", ("hmul", sl, hcp), i)

            def em_dn(hcp):
                C.wt("tensor", "dve", "hmul", sl, hcp)
                C.wt("tensor", "gld", "w_d")
                if hcp == 0 and sl >= 1:
                    for q2 in range(2):
                        C.wt("tensor", "dve", "fapply", sl - 1, q2)
                if C.planning():
                    C.ms("tensor", "pe", ("dn", sl, hcp))
                elif C.on("tensor"):
                    p3 = hcp % 3
                    for j in range(2):
                        hc = 2 * hcp + j
                        for tt2 in range(2):
                            lhsT = hfm_s[:, p3 * 512 + j * 256 + tt2 * 128:
                                         p3 * 512 + j * 256 + (tt2 + 1) * 128]
                            for b in range(2):
                                i = C.E.matmul(
                                    pdn[tt2][:, b * 512:(b + 1) * 512], lhsT,
                                    wd_s[:, hc * DM + b * 512:
                                         hc * DM + (b + 1) * 512],
                                    start=(hc == 0), stop=(hc == HC - 1),
                                    skip_group_check=True)
                    C.ms("tensor", "pe", ("dn", sl, hcp), i)

            for j in range(HC // 2 + 2):
                if sl < NSL and j < HC // 2:
                    em_gu(j)
                if j == 0 and sl >= 1:
                    em_chain_a(sl - 1)
                if j == 1:
                    if sl >= 1:
                        em_chain_b(sl - 1)
                    if 1 <= sl < NSL:
                        em_x2l(sl)
                if sl < NSL and j >= 2:
                    em_dn(j - 2)
            if sl + 1 < NSL:
                em_l2sl(sl + 1)
        C.snap("ffn_end")

    def ph_final(C):
        if C.planning():
            C.snap("final")
            return
        if C.on("sync"):
            final = C.vals[("snap", "final")]
            for s in SEMS:
                v = final.get(s, 0)
                if v > 0:
                    C.E.wait_ge(C.sems[s], v)

    def ph_debug(C):
        if not debug:
            return
        taps = []
        if stage < 4:
            taps += [("dbg_qfm", qfm_s), ("dbg_kfm", kfm_s), ("dbg_vtm", vtm_s)]
        if 2 <= stage <= 4:
            taps += [("dbg_ofm", ofm_s)]
        if C.planning():
            for j in range(len(taps)):
                C.ms("gp", "dbg", ("dbg", j), k=16)
            return
        if C.on("gp"):
            names = {"dbg_qfm": dbg_qfm, "dbg_kfm": dbg_kfm,
                     "dbg_vtm": dbg_vtm, "dbg_ofm": dbg_ofm}
            final = C.vals[("snap", "final")]
            for s in ["pe", "dve", "act", "sto0", "sto1"]:
                v = final.get(s, 0)
                if v > 0:
                    C.E.wait_ge(C.sems[s], v)
            for j, (nm, src) in enumerate(taps):
                C.ms("gp", "dbg", ("dbg", j),
                     C.E.dma_start(out=names[nm].ap(), in_=src[:, :]), k=16)
            if taps:
                C.E.wait_ge(C.sems["dbg"], C.at("dbg", "dbg", len(taps) - 1))

    def walk(C, P):
        ph_loads(C)
        if stage >= 1:
            ph_qkv(C, 0, P.get("pq"), P.get("ptpl"), P.get("ptpq"))
        if stage >= 2:
            ph_attn(C, 0, P.get("pS"), P.get("pO"))
        if stage >= 3:
            ph_qkv(C, 1, P.get("pq"), P.get("ptpl"), P.get("ptpq"))
            ph_attn(C, 1, P.get("pS"), P.get("pO"), P.get("pp"),
                    with_proj=(stage >= 4))
        if stage >= 5:
            ph_ffn(C, P.get("pgu"), P.get("pdn"))
        ph_final(C)
        ph_debug(C)

    walk(C, {})

    es = ExitStack()
    sems = {n: es.enter_context(nc.semaphore("sem_" + n)) for n in SEMS}
    C.sems = sems
    ENGS = [("sync", "sync"), ("gp", "gpsimd"), ("tensor", "tensor"),
            ("vector", "vector"), ("scalar", "scalar")]

    with es, nc.Block() as block:
        def emit(fn, *args):
            for mode, attr in ENGS:
                C.mode = mode

                def body(E, fn=fn, args=args):
                    C.E = E
                    fn(C, *args)
                getattr(block, attr)(body)

        def qkv_psums(pes, tag):
            pq = [pes.enter_context(nc.psum_tensor(f"pq{i}_{tag}", [128, 1024], F32))
                  for i in range(3)]
            ptpl = pes.enter_context(nc.psum_tensor(f"ptpl_{tag}", [128, 1024], BF16))
            ptpq = pes.enter_context(nc.psum_tensor(f"ptpq_{tag}", [128, 512], BF16))
            return pq, ptpl, ptpq

        def attn_psums(pes, tag):
            pS = [pes.enter_context(nc.psum_tensor(f"pS{i}_{tag}", [128, 1024], F32))
                  for i in range(3)]
            pO = pes.enter_context(nc.psum_tensor(f"pO_{tag}", [128, 1024], F32))
            return pS, pO

        emit(ph_loads)
        if stage >= 1:
            with ExitStack() as pes:
                pq, ptpl, ptpq = qkv_psums(pes, "a")
                emit(ph_qkv, 0, pq, ptpl, ptpq)
        if stage >= 2:
            with ExitStack() as pes:
                pS, pO = attn_psums(pes, "a")
                emit(ph_attn, 0, pS, pO)
        if stage >= 3:
            with ExitStack() as pes:
                pq, ptpl, ptpq = qkv_psums(pes, "b")
                emit(ph_qkv, 1, pq, ptpl, ptpq)
            with ExitStack() as pes:
                nps = 2 if stage >= 4 else 3
                pS = [pes.enter_context(
                    nc.psum_tensor(f"pS{i}_b", [128, 1024], F32))
                    for i in range(nps)]
                pO = pes.enter_context(nc.psum_tensor("pO_b", [128, 1024], F32))
                pp = None
                if stage >= 4:
                    pp = [pes.enter_context(
                        nc.psum_tensor("pp0", [128, 1024], F32))]
                emit(ph_attn, 1, pS, pO, pp, stage >= 4)
        if stage >= 5:
            with ExitStack() as pes:
                pgu = [pes.enter_context(nc.psum_tensor(f"pgu{i}", [128, 1024],
                                                        F32)) for i in range(2)]
                pdn = [pes.enter_context(nc.psum_tensor(f"pdn{i}", [128, 1024],
                                                        F32)) for i in range(2)]
                emit(ph_ffn, pgu, pdn)
        emit(ph_final)
        emit(ph_debug)

    return nc


def _perm_cols():
    return np.concatenate([np.arange(0, D, 2), np.arange(1, D, 2)])


_TAB_CACHE = {}


def _rope_tables(T):
    if T in _TAB_CACHE:
        return _TAB_CACHE[T]
    try:
        import jax
        import jax.numpy as jnp
        with jax.default_device(jax.devices("cpu")[0]):
            pos = jnp.arange(T, dtype=jnp.float32)
            inv = 1.0 / 10000.0 ** (jnp.arange(0, D, 2, dtype=jnp.float32) / D)
            fr = pos[:, None] * inv[None, :]
            cos = np.asarray(jnp.cos(fr), dtype=np.float32)
            sin = np.asarray(jnp.sin(fr), dtype=np.float32)
    except Exception:
        posn = np.arange(T, dtype=np.float32)
        invn = (1.0 / 10000.0 ** (np.arange(0, D, 2, dtype=np.float32) / D)
                ).astype(np.float32)
        frn = posn[:, None] * invn[None, :]
        cos = np.cos(frn).astype(np.float32)
        sin = np.sin(frn).astype(np.float32)
    _TAB_CACHE[T] = (cos, sin)
    return cos, sin


def _rope_packed(positions, T_full):
    cos_f, sin_f = _rope_tables(T_full)
    cos = cos_f[positions]
    sin = sin_f[positions]
    return np.ascontiguousarray(np.concatenate([cos, cos, -sin, sin], axis=1))


def _mask_bank(z):
    tri = np.tril(np.full((128, 128), 1.0, dtype=np.float32), k=-1)
    bank = np.zeros((128, 8, 512), dtype=np.float32)
    for jj in range(8):
        if z == 0:
            if jj < 4:
                o = 128 * jj
                bank[:, jj, :o] = 1.0
                bank[:, jj, o:o + 128] = tri
            else:
                bank[:, jj, :] = 1.0
        else:
            if jj >= 4:
                o = 128 * (jj - 4)
                bank[:, jj, :o] = 1.0
                bank[:, jj, o:o + 128] = tri
    return np.ascontiguousarray(1.0 - bank.reshape(128, 8 * 512))


def _own_rows(z, T):
    blocks = np.arange(z, T // 512, 2)
    return (blocks[:, None] * 512 + np.arange(512)[None, :]).reshape(-1)


def make_in_maps(inputs, T=4096, n_cores=8):
    x = np.asarray(inputs["L0_attn_x"], dtype=np.float32)
    Wqkv = np.asarray(inputs["L0_attn_Wqkv"], dtype=np.float32)
    bqkv = np.asarray(inputs["L0_attn_bqkv"], dtype=np.float32)
    Wproj = np.asarray(inputs["L0_attn_Wproj"], dtype=np.float32)
    bproj = np.asarray(inputs["L0_attn_bproj"], dtype=np.float32)
    wg = np.asarray(inputs["L0_ffn_w_gate"]).astype(np.float32)
    wu = np.asarray(inputs["L0_ffn_w_up"]).astype(np.float32)
    wd = np.asarray(inputs["L0_ffn_w_down"]).astype(np.float32)

    perm = _perm_cols()
    cols = []
    for g in range(2):
        heads = np.arange(g * 8, (g + 1) * 8)
        qcols = np.concatenate([h * D + perm for h in heads])
        kcols = np.concatenate([DM + h * D + perm for h in heads])
        vcols = np.concatenate([2 * DM + h * D + np.arange(D) for h in heads])
        cols.append(np.concatenate([qcols, kcols, vcols]))
    cols = np.concatenate(cols)
    wqkv_c = np.ascontiguousarray(Wqkv[:, cols])
    bqkv_c = np.ascontiguousarray(bqkv[cols])[None, :]

    ropek = _rope_packed(np.arange(T), T)
    wgT = np.ascontiguousarray(wg.T)
    wuT = np.ascontiguousarray(wu.T)
    wdT = np.ascontiguousarray(wd.T)
    ident = np.eye(128, dtype=np.float32)

    in_maps = []
    for c in range(n_cores):
        b, z = c // 2, c % 2
        rows = _own_rows(z, T)
        in_maps.append({
            "x": np.ascontiguousarray(x[b]),
            "x_own": np.ascontiguousarray(x[b][rows]),
            "wqkv": wqkv_c, "bqkv": bqkv_c,
            "wproj": Wproj, "bproj": np.ascontiguousarray(bproj)[None, :],
            "wg": wgT, "wu": wuT, "wd": wdT,
            "ropek": ropek,
            "ropeq": np.ascontiguousarray(ropek[rows]),
            "maskb": _mask_bank(z),
            "ident": ident,
        })
    return in_maps


_NC_CACHE = {}


def kernel(**inputs):
    T = 4096
    n_cores = 8
    key = (T, n_cores)
    if key not in _NC_CACHE:
        _NC_CACHE[key] = build_nc(T=T, n_cores=n_cores, stage=5, debug=False)
    nc = _NC_CACHE[key]
    in_maps = make_in_maps(inputs, T=T, n_cores=n_cores)
    res = run_bass_kernel_spmd(nc, in_maps, core_ids=list(range(n_cores)))
    B = np.asarray(inputs["L0_attn_x"]).shape[0]
    out = np.empty((B, T, DM), dtype=np.float32)
    for c in range(n_cores):
        b, z = c // 2, c % 2
        out[b, _own_rows(z, T)] = res.results[c]["out"]
    return out


# version 30
# speedup vs baseline: 1.0166x; 1.0166x over previous
import numpy as np
import ml_dtypes
from contextlib import ExitStack
import concourse.bass as bass
import concourse.mybir as mybir
from concourse.bass_utils import run_bass_kernel_spmd

F32 = mybir.dt.float32
BF16 = mybir.dt.bfloat16
AF = mybir.ActivationFunctionType
ALU = mybir.AluOpType

D = 64
DM = 1024
DC = 8
FH = 3072
HC = 24
NEG = -1.0e30
C127 = float(np.float32(1.0) / np.float32(127.0))
EPS = 1e-5
EPS_FFN = float(1e-5 / np.float64(C127) ** 4)
F0 = np.float32(0.0)
F1 = np.float32(1.0)
FH2 = np.float32(-0.5)


class _Plan:

    def __init__(self):
        self.mode = "plan"
        self.vals = {}
        self.ctr = {}
        self.E = None
        self.sems = None

    def planning(self):
        return self.mode == "plan"

    def on(self, eng):
        return self.mode == eng

    def ms(self, eng, sem, key, inst=None, k=1):
        if self.planning():
            v = self.ctr.get(sem, 0) + k
            self.ctr[sem] = v
            assert (sem,) + key not in self.vals, ("dup milestone", sem, key)
            self.vals[(sem,) + key] = v
        elif self.on(eng):
            assert inst is not None, ("missing inst", sem, key)
            inst.then_inc(self.sems[sem], k)

    def at(self, sem, *key):
        return self.vals[(sem,) + key]

    def wt(self, eng, sem, *key):
        if self.planning():
            assert ((sem,) + key) in self.vals, ("wait on unplanned", sem, key)
        elif self.on(eng):
            self.E.wait_ge(self.sems[sem], self.vals[(sem,) + key])

    def snap(self, name):
        if self.planning():
            self.vals[("snap", name)] = dict(self.ctr)

    def fence(self, eng, name, sems):
        if self.planning():
            assert ("snap", name) in self.vals
            return
        if self.on(eng):
            snapshot = self.vals[("snap", name)]
            for s in sems:
                v = snapshot.get(s, 0)
                if v > 0:
                    self.E.wait_ge(self.sems[s], v)


def build_nc(T=4096, n_cores=8, stage=5, debug=False):
    NT = T // 128
    TH = T // 2
    NQT = TH // 128
    NB = T // 512
    NSLOT = NB // 2
    NSL = TH // 256
    NU = NQT + NT

    def csched(s):
        return (2 * s + 2) * 4

    nc = bass.Bass("TRN2", target_bir_lowering=False, debug=False)

    x_d = nc.dram_tensor("x", [T, DM], F32, kind="ExternalInput")
    xown_d = nc.dram_tensor("x_own", [TH, DM], F32, kind="ExternalInput")
    wqkv_d = nc.dram_tensor("wqkv", [DM, 2 * 1536], F32, kind="ExternalInput")
    bqkv_d = nc.dram_tensor("bqkv", [1, 2 * 1536], F32, kind="ExternalInput")
    wproj_d = nc.dram_tensor("wproj", [DM, DM], F32, kind="ExternalInput")
    bproj_d = nc.dram_tensor("bproj", [1, DM], F32, kind="ExternalInput")
    wg_d = nc.dram_tensor("wg", [DM, FH], F32, kind="ExternalInput")
    wu_d = nc.dram_tensor("wu", [DM, FH], F32, kind="ExternalInput")
    wd_d = nc.dram_tensor("wd", [FH, DM], F32, kind="ExternalInput")
    ropek_d = nc.dram_tensor("ropek", [T, 128], F32, kind="ExternalInput")
    ropeq_d = nc.dram_tensor("ropeq", [TH, 128], F32, kind="ExternalInput")
    maskb_d = nc.dram_tensor("maskb", [128, 8 * 512], F32, kind="ExternalInput")
    ident_d = nc.dram_tensor("ident", [128, 128], F32, kind="ExternalInput")

    out_d = nc.dram_tensor("out", [TH, DM], F32, kind="ExternalOutput")

    dkind = "ExternalOutput" if debug else "Internal"
    x2_d = nc.dram_tensor("x2s", [TH, DM], F32, kind=dkind)
    ln2_d = nc.dram_tensor("ln2s", [DM, TH], BF16, kind=dkind)
    wcat_d = nc.dram_tensor("wcat", [128, 3 * 24576], BF16, kind="Internal")
    if debug:
        dbg_qfm = nc.dram_tensor("dbg_qfm", [128, 4 * TH], F32, kind="ExternalOutput")
        dbg_kfm = nc.dram_tensor("dbg_kfm", [128, 4 * T], F32, kind="ExternalOutput")
        dbg_vtm = nc.dram_tensor("dbg_vtm", [128, NT * 520], F32, kind="ExternalOutput")
        dbg_ofm = nc.dram_tensor("dbg_ofm", [128, 8 * TH], F32, kind="ExternalOutput")

    SB_BASE = 16384

    def sb(name, shape, dt, off):
        return nc.alloc_sbuf_tensor_at(name, shape, dt, offset=SB_BASE + off)

    ident_s = sb("ident_s", [128, 128], BF16, 0)
    onesf_s = sb("onesf_s", [128, 64], F32, 256)
    ones_b = sb("ones_b", [128, 512], BF16, 512)
    eps_s = sb("eps_s", [128, 8], F32, 1536)
    bqkv_s = sb("bqkv_s", [1, 2 * 1536], BF16, 1568)
    bproj_s = sb("bproj_s", [1, DM], BF16, 7712)
    maskb_s = sb("maskb_s", [128, 8 * 512], BF16, 9760)
    identf_s = sb("identf_s", [128, 128], F32, 17952)
    O_KVQ = 18464

    kfm_s = sb("kfm", [128, 4 * T], BF16, O_KVQ)
    vtm_s = sb("vtm", [128, NT * 520], BF16, O_KVQ + 2 * 4 * T)
    qfm_s = sb("qfm", [128, 4 * TH], BF16, O_KVQ + 2 * 4 * T + 2 * NT * 520)
    O_LG1 = O_KVQ + 2 * 4 * T + 2 * NT * 520 + 2 * 4 * TH

    wqkv_s = sb("wqkv_s", [128, DC * 1536], BF16, O_LG1)
    wstg_s = sb("wstg", [128, 2 * 3072], BF16, O_LG1)
    tabk_s = sb("tabk", [128, 3 * 128], F32, O_LG1 + 24576)
    tabq_s = sb("tabq", [128, 3 * 128], F32, O_LG1 + 26112)
    xbuf_s = sb("xbuf", [128, 3 * DM], F32, O_LG1 + 27648)
    ln1tm_s = sb("ln1tm", [128, 3 * DM], BF16, O_LG1 + 39936)
    ln1fm_s = sb("ln1fm", [128, 3 * DM], BF16, O_LG1 + 46080)
    tmpcc_s = sb("tmpcc", [128, 512], F32, O_LG1 + 52224)
    swe_s = sb("swe", [128, 256], F32, O_LG1 + 54272)
    swo_s = sb("swo", [128, 256], F32, O_LG1 + 55296)
    bn_s = sb("bn", [128, 48], F32, O_LG1 + 56320)
    st1_s = sb("st1", [128, 24], F32, O_LG1 + 56512)
    O_OFM = O_LG1 + 56608

    ofm_s = sb("ofm", [128, 8 * TH], BF16, O_OFM)
    O_D = O_OFM + 2 * 8 * TH

    pt_s = sb("pt", [128, 3 * 1024], BF16, O_D)
    stage_s = sb("stage", [128, 4 * 1024], BF16, O_D + 6144)
    den_s = sb("den", [128, 1024], F32, O_D + 14336)
    rs_s = sb("rs", [128, 1024], BF16, O_D + 18432)
    bn2_s = sb("bn2", [128, 32], F32, O_D + 20480)
    st2_s = sb("st2", [128, 16], F32, O_D + 20608)
    O_END_D = O_D + 20672

    overlay = (O_KVQ + 98304 + 16384 <= O_OFM
               and O_KVQ + 98304 + 16384 + 49152 <= O_END_D + 8192)
    if overlay:
        O_WG, O_WU = O_KVQ, O_KVQ + 49152
        O_WP = O_KVQ + 98304
        O_WD = O_WP + 16384
        O_FFNS = O_WD + 49152
    else:
        O_WG = O_KVQ
        O_WD = O_KVQ + 49152
        O_FFNS = O_WD + 49152
        O_WU = O_FFNS + 29952
        O_WP = O_WU + 49152
    wg_s = sb("wg_s", [128, DC * FH], BF16, O_WG)
    wu_s = sb("wu_s", [128, DC * FH], BF16, O_WU)
    wproj_s = sb("wproj_s", [128, DC * DM], BF16, O_WP)
    wd_s = sb("wd_s", [128, HC * DM], BF16, O_WD)
    ybuf_s = sb("ybuf", [128, 2 * DM], F32, O_WD)
    ln2f_s = sb("ln2f", [128, DM], F32, O_WD + 8192)
    ln2b_s = sb("ln2b", [128, 2 * DM], BF16, O_WD + 12288)
    xo_s = sb("xo", [128, DM], F32, O_WD + 16384)
    assert O_WD + 20480 <= O_OFM, (O_WD, O_OFM)
    ln2sl_s = sb("ln2sl", [128, 2 * DC * 256], BF16, O_FFNS)
    gs_s = sb("gs", [128, 2 * 512], BF16, O_FFNS + 8192)
    hfm_s = sb("hfm", [128, 3 * 512], BF16, O_FFNS + 10240)
    x2t_s = sb("x2t", [128, 2 * DM], F32, O_FFNS + 13312)
    yffn_s = sb("yffn", [128, 2 * DM], F32, O_FFNS + 21504)
    bn3_s = sb("bn3", [128, 32], F32, O_FFNS + 29696)
    st3_s = sb("st3", [128, 16], F32, O_FFNS + 29824)
    assert SB_BASE + max(O_END_D, O_FFNS + 29888) <= 229344, (O_END_D, O_FFNS)

    C = _Plan()
    SEMS = (["gld", "pe", "act", "dve", "dbg"]
            + [f"ldx{i}" for i in range(3)] + [f"ldt{i}" for i in range(3)]
            + [f"xo{i}" for i in range(2)] + [f"l2sl{i}" for i in range(2)]
            + [f"x2l{i}" for i in range(2)] + [f"sto{i}" for i in range(2)]
            + [f"x2st{i}" for i in range(2)] + [f"l2st{i}" for i in range(2)]
            + [f"outst{i}" for i in range(2)]
            + [f"wc{i}" for i in range(2)] + [f"ws{i}" for i in range(2)]
            + ["wub"])

    def ph_loads(C):
        if C.planning():
            for k in ["w_ident", "w_identf", "w_bqkv", "w_bproj", "w_mask",
                      "w_qkv0"]:
                C.ms("gp", "gld", (k,), k=16)
            C.ms("vector", "dve", ("memset",))
            return
        if C.on("gp"):
            C.ms("gp", "gld", ("w_ident",), C.E.dma_start(
                out=ident_s[:, :], in_=ident_d.ap()), k=16)
            C.ms("gp", "gld", ("w_identf",), C.E.dma_start(
                out=identf_s[:, :], in_=ident_d.ap()), k=16)
            C.ms("gp", "gld", ("w_bqkv",), C.E.dma_start(
                out=bqkv_s[:, :], in_=bqkv_d.ap()), k=16)
            C.ms("gp", "gld", ("w_bproj",), C.E.dma_start(
                out=bproj_s[:, :], in_=bproj_d.ap()), k=16)
            C.ms("gp", "gld", ("w_mask",), C.E.dma_start(
                out=maskb_s[:, :], in_=maskb_d.ap()), k=16)
            C.ms("gp", "gld", ("w_qkv0",), C.E.dma_start(
                out=wqkv_s[:, :].rearrange("p (c f) -> p c f", f=1536),
                in_=wqkv_d.ap()[:, 0:1536].rearrange("(c p) f -> p c f", p=128)),
                k=16)
        elif C.on("vector"):
            C.E.memset(ones_b[:, :], 1.0)
            C.E.memset(onesf_s[:, :], 1.0)
            C.E.memset(eps_s[:, 0:1], EPS)
            C.E.memset(eps_s[:, 1:2], EPS_FFN)
            C.E.memset(ofm_s[:, :], 0.0)
            i = C.E.memset(vtm_s[:, :], 1.0)
            C.ms("vector", "dve", ("memset",), i)

    def ph_qkv(C, g2, pq, ptpl, ptpq):
        wq_key = "w_qkv1" if g2 == 1 else "w_qkv0"
        if g2 == 1:
            if C.planning():
                C.ms("gp", "gld", ("w_qkv1",), k=16)
            elif C.on("gp"):
                C.E.wait_ge(C.sems["pe"], C.vals[("snap", "qkv0_end")]["pe"])
                C.E.wait_ge(C.sems["gld"], C.at("gld", "w_qkv0"))
                C.ms("gp", "gld", ("w_qkv1",), C.E.dma_start(
                    out=wqkv_s[:, :].rearrange("p (c f) -> p c f", f=1536),
                    in_=wqkv_d.ap()[:, 1536:3072].rearrange("(c p) f -> p c f",
                                                            p=128)), k=16)
            C.fence("tensor", "attn0_end", ["act", "dve"])
            C.fence("scalar", "attn0_end", ["pe", "dve"])
            C.fence("vector", "attn0_end", ["pe", "act"])
            C.fence("sync", "attn0_end", ["act", "dve", "pe"])

        def st_load(u):
            p = u % 3
            isq = u < NQT
            t = u if isq else u - NQT
            key = (g2, u)
            xb = xbuf_s[:, p * DM:(p + 1) * DM]
            if u >= 3:
                C.wt("sync", "act", "apply", g2, u - 3)
            if C.planning():
                C.ms("sync", f"ldx{u % 3}", ("x", *key), k=16)
            elif C.on("sync"):
                src = (xown_d if isq else x_d).ap()[t * 128:(t + 1) * 128, :]
                C.ms("sync", f"ldx{u % 3}", ("x", *key),
                     C.E.dma_start(out=xb, in_=src), k=16)

        def st_ldtab(u):
            p = u % 3
            isq = u < NQT
            t = u if isq else u - NQT
            key = (g2, u)
            if u >= 3:
                C.wt("sync", "dve", "rope", g2, u - 3)
            if C.planning():
                C.ms("sync", f"ldt{u % 3}", ("tb", *key), k=16)
            elif C.on("sync"):
                tb = (tabq_s if isq else tabk_s)[:, p * 128:(p + 1) * 128]
                src = (ropeq_d if isq else ropek_d).ap()[t * 128:(t + 1) * 128, :]
                C.ms("sync", f"ldt{u % 3}", ("tb", *key),
                     C.E.dma_start(out=tb, in_=src), k=16)

        def st_stats(u):
            p = u % 3
            key = (g2, u)
            xb = xbuf_s[:, p * DM:(p + 1) * DM]
            stm = st1_s[:, p * 8:p * 8 + 8]
            C.wt("vector", f"ldx{u % 3}", "x", *key)
            if u >= 3:
                C.wt("vector", "act", "apply", g2, u - 3)
            if C.planning():
                C.ms("vector", "dve", ("stats", *key))
            elif C.on("vector"):
                C.E.drain()
                C.E.bn_stats(bn_s[:, p * 16:p * 16 + 6], xb[:, 0:512])
                C.E.bn_stats(bn_s[:, p * 16 + 6:p * 16 + 12], xb[:, 512:1024])
                C.E.drain()
                i = C.E.bn_aggr(stm[:, 0:2], bn_s[:, p * 16:p * 16 + 12])
                C.ms("vector", "dve", ("stats", *key), i)

        def st_rstd(u):
            p = u % 3
            key = (g2, u)
            stm = st1_s[:, p * 8:p * 8 + 8]
            C.wt("scalar", "dve", "stats", *key)
            C.wt("scalar", "dve", "memset")
            if C.planning():
                C.ms("scalar", "act", ("rstd", *key))
            elif C.on("scalar"):
                C.E.activation(stm[:, 2:3], stm[:, 1:2], AF.Ln,
                               bias=eps_s[:, 0:1], scale=F1)
                C.E.drain()
                i = C.E.activation(stm[:, 3:4], stm[:, 2:3], AF.Exp,
                                   bias=F0, scale=FH2)
                C.ms("scalar", "act", ("rstd", *key), i)

        def st_nmr(u):
            p = u % 3
            key = (g2, u)
            stm = st1_s[:, p * 8:p * 8 + 8]
            C.wt("vector", "act", "rstd", *key)
            if C.planning():
                C.ms("vector", "dve", ("nmr", *key))
            elif C.on("vector"):
                i = C.E.tensor_scalar(stm[:, 4:5], stm[:, 0:1], stm[:, 3:4],
                                      -1.0, ALU.mult, ALU.mult)
                C.ms("vector", "dve", ("nmr", *key), i)

        def st_apply(u):
            p = u % 3
            key = (g2, u)
            stm = st1_s[:, p * 8:p * 8 + 8]
            xb = xbuf_s[:, p * DM:(p + 1) * DM]
            C.wt("scalar", "dve", "nmr", *key)
            if u >= 3:
                C.wt("scalar", "pe", "tpl", g2, u - 3)
            if C.planning():
                C.ms("scalar", "act", ("apply", *key))
            elif C.on("scalar"):
                i = C.E.activation(ln1tm_s[:, p * DM:(p + 1) * DM], xb,
                                   AF.Identity, bias=stm[:, 4:5],
                                   scale=stm[:, 3:4])
                C.ms("scalar", "act", ("apply", *key), i)

        def st_tpl(u):
            p = u % 3
            key = (g2, u)
            C.wt("tensor", "act", "apply", *key)
            C.wt("tensor", "gld", "w_qkv0")
            if u >= 1:
                C.wt("tensor", "act", "evl", g2, u - 1)
            if C.planning():
                C.ms("tensor", "pe", ("tpl", *key))
            elif C.on("tensor"):
                for c in range(8):
                    i = C.E.transpose(
                        ptpl[:, c * 128:(c + 1) * 128],
                        ln1tm_s[:, p * DM + c * 128:p * DM + (c + 1) * 128],
                        ident_s[:, :])
                C.ms("tensor", "pe", ("tpl", *key), i)

        def st_evl(u):
            p = u % 3
            key = (g2, u)
            C.wt("scalar", "pe", "tpl", *key)
            if u >= 3:
                C.wt("scalar", "pe", "tpq", g2, u - 3)
                C.wt("scalar", "dve", "rope", g2, u - 3)
            if C.planning():
                C.ms("scalar", "act", ("evl", *key))
            elif C.on("scalar"):
                i = C.E.activation(ln1fm_s[:, p * DM:(p + 1) * DM], ptpl[:, :],
                                   AF.Identity, bias=F0, scale=F1)
                C.ms("scalar", "act", ("evl", *key), i)

        def st_qkv(u):
            p = u % 3
            isq = u < NQT
            key = (g2, u)
            C.wt("tensor", "act", "evl", *key)
            C.wt("tensor", "gld", wq_key)
            if u >= 3:
                C.wt("tensor", "dve", "rope", g2, u - 3)
                if u - 3 >= NQT:
                    C.wt("tensor", "act", "vev", g2, (u - 3) - NQT)
            if C.planning():
                C.ms("tensor", "pe", ("qkv", *key))
            elif C.on("tensor"):
                boff = 0 if isq else 512
                nb = 1 if isq else 2
                bph = g2 * 1536
                for b in range(nb):
                    C.E.matmul(pq[p][:, b * 512:(b + 1) * 512], ones_b[0:1, 0:128],
                               bqkv_s[0:1, bph + boff + b * 512:
                                      bph + boff + (b + 1) * 512],
                               start=True, stop=False, skip_group_check=True)
                for c in range(DC):
                    lhsT = ln1fm_s[:, p * DM + c * 128:p * DM + (c + 1) * 128]
                    for b in range(nb):
                        i = C.E.matmul(
                            pq[p][:, b * 512:(b + 1) * 512], lhsT,
                            wqkv_s[:, c * 1536 + boff + b * 512:
                                   c * 1536 + boff + (b + 1) * 512],
                            start=False, stop=(c == DC - 1), skip_group_check=True)
                C.ms("tensor", "pe", ("qkv", *key), i)

        def st_rope(u):
            p = u % 3
            isq = u < NQT
            key = (g2, u)
            C.wt("vector", "pe", "qkv", *key)
            C.wt("vector", f"ldt{u % 3}", "tb", *key)
            if C.planning():
                C.ms("vector", "dve", ("rope", *key))
            elif C.on("vector"):
                qk = pq[p][:, 0:512]
                tb = (tabq_s if isq else tabk_s)[:, p * 128:(p + 1) * 128]
                r3 = lambda ap, s_: ap.rearrange("p (h s) -> p h s", s=s_)
                ev = qk.rearrange("p (h two s) -> p h two s", two=2, s=32)[:, :, 0, :]
                ov = qk.rearrange("p (h two s) -> p h two s", two=2, s=32)[:, :, 1, :]
                cc = tb[:, 0:64].unsqueeze(1).broadcast_to([128, 8, 64]) \
                    .rearrange("p h (two s) -> p h two s", two=2)
                sn = tb[:, 64:96].unsqueeze(1).broadcast_to([128, 8, 32])
                sp = tb[:, 96:128].unsqueeze(1).broadcast_to([128, 8, 32])
                C.E.drain()
                C.E.tensor_tensor(
                    tmpcc_s[:, 0:512].rearrange("p (h two s) -> p h two s",
                                                two=2, s=32),
                    qk.rearrange("p (h two s) -> p h two s", two=2, s=32),
                    cc, ALU.mult)
                C.E.tensor_tensor(r3(swe_s[:, 0:256], 32), ov, sn, ALU.mult)
                C.E.tensor_tensor(r3(swo_s[:, 0:256], 32), ev, sp, ALU.mult)
                C.E.drain()
                tme = tmpcc_s[:, 0:512].rearrange(
                    "p (h two s) -> p h two s", two=2, s=32)[:, :, 0, :]
                tmo = tmpcc_s[:, 0:512].rearrange(
                    "p (h two s) -> p h two s", two=2, s=32)[:, :, 1, :]
                dst = ln1fm_s[:, p * DM:p * DM + 512].rearrange(
                    "p (h two s) -> p h two s", two=2, s=32)
                C.E.tensor_tensor(dst[:, :, 0, :], tme, r3(swe_s[:, 0:256], 32),
                                  ALU.add)
                i = C.E.tensor_tensor(dst[:, :, 1, :], tmo, r3(swo_s[:, 0:256], 32),
                                      ALU.add)
                C.ms("vector", "dve", ("rope", *key), i)

        def st_vev(u):
            p = u % 3
            t = u - NQT
            key = (g2, u)
            C.wt("scalar", "pe", "qkv", *key)
            if C.planning():
                C.ms("scalar", "act", ("vev", g2, t))
            elif C.on("scalar"):
                dst = vtm_s[:, t * 520:(t + 1) * 520].rearrange(
                    "p (h s) -> p h s", s=65)[:, :, 0:64]
                i = C.E.activation(
                    dst, pq[p][:, 512:1024].rearrange("p (h s) -> p h s", s=64),
                    AF.Identity, bias=F0, scale=F1)
                C.ms("scalar", "act", ("vev", g2, t), i)

        def st_tpq(u):
            p = u % 3
            key = (g2, u)
            C.wt("tensor", "dve", "rope", *key)
            if u >= 1:
                C.wt("tensor", "dve", "evq", g2, u - 1)
            if C.planning():
                C.ms("tensor", "pe", ("tpq", *key))
            elif C.on("tensor"):
                src = ln1fm_s[:, p * DM:p * DM + 512]
                for c in range(4):
                    i = C.E.transpose(ptpq[:, c * 128:(c + 1) * 128],
                                      src[:, c * 128:(c + 1) * 128], ident_s[:, :])
                C.ms("tensor", "pe", ("tpq", *key), i)

        def st_evq(u):
            isq = u < NQT
            t = u if isq else u - NQT
            key = (g2, u)
            C.wt("vector", "pe", "tpq", *key)
            if C.planning():
                C.ms("vector", "dve", ("evq", *key))
            elif C.on("vector"):
                if isq:
                    dst = qfm_s[:, :].rearrange("p (h n f) -> p h n f", h=4, n=NQT)
                else:
                    dst = kfm_s[:, :].rearrange("p (h n f) -> p h n f", h=4, n=NT)
                i = C.E.tensor_copy(dst[:, :, t, :],
                                    ptpq[:, 0:512].rearrange("p (h f) -> p h f", h=4))
                C.ms("vector", "dve", ("evq", *key), i)

        for v in range(-1, NU + 3):
            if v + 1 < NU:
                st_load(v + 1)
                st_stats(v + 1)
                st_rstd(v + 1)
                st_nmr(v + 1)
            if 0 <= v < NU:
                st_ldtab(v)
                st_apply(v)
            if 0 <= v - 1 < NU:
                st_tpl(v - 1)
                st_evl(v - 1)
            if 0 <= v - 2 < NU:
                st_qkv(v - 2)
                st_rope(v - 2)
                if v - 2 >= NQT:
                    st_vev(v - 2)
            if 0 <= v - 3 < NU:
                st_tpq(v - 3)
                st_evq(v - 3)
        C.snap(f"qkv{g2}_end")

    def ph_attn(C, g2, pS, pO, pp=None, with_proj=False):
        NPS = 2 if with_proj else 3
        C.fence("tensor", f"qkv{g2}_end", ["dve", "act"])
        C.fence("scalar", f"qkv{g2}_end", ["pe", "dve"])
        C.fence("vector", f"qkv{g2}_end", ["pe", "act", "gld"])
        if C.planning():
            C.ms("vector", "dve", ("amemset", g2))
        elif C.on("vector"):
            C.E.drain()
            for i0 in range(NPS):
                C.E.memset(pS[i0][:, :], 0.0)
            i = C.E.memset(pO[:, :], 0.0)
            C.ms("vector", "dve", ("amemset", g2), i)
        if g2 == 1:
            if C.planning():
                C.ms("gp", "gld", ("w_proj",), k=16)
            elif C.on("gp"):
                snapshot = C.vals[("snap", "qkv1_end")]
                for s in ["pe", "dve", "act"]:
                    C.E.wait_ge(C.sems[s], snapshot[s])
                C.E.wait_ge(C.sems["gld"], C.at("gld", "w_qkv1"))
                C.ms("gp", "gld", ("w_proj",), C.E.dma_start(
                    out=wproj_s[:, :].rearrange("p (c f) -> p c f", f=DM),
                    in_=wproj_d.ap().rearrange("(c p) f -> p c f", p=128)), k=16)

        def em_bounce(i):
            q = i % 2
            w, j = divmod(i, 8)
            if i >= 2:
                C.wt("gp", f"ws{q}", "wstr", i - 2)
            if C.planning():
                C.ms("gp", f"wc{q}", ("wcst", i), k=16)
            elif C.on("gp"):
                stg = wstg_s[:, q * 3072:(q + 1) * 3072]
                if w < 2:
                    src_d = wg_d if w == 0 else wu_d
                    dma = C.E.dma_start(
                        out=stg.rearrange("p (c f) -> p c f", f=FH),
                        in_=src_d.ap().rearrange("(c p) f -> p c f", p=128)
                        [:, j:j + 1, :])
                else:
                    dma = C.E.dma_start(
                        out=stg.rearrange("p (c f) -> p c f", f=DM),
                        in_=wd_d.ap().rearrange("(c p) f -> p c f", p=128)
                        [:, 3 * j:3 * j + 3, :])
                C.ms("gp", f"wc{q}", ("wcst", i), dma, k=16)
            C.wt("gp", f"wc{q}", "wcst", i)
            if C.planning():
                C.ms("gp", f"ws{q}", ("wstr", i), k=16)
            elif C.on("gp"):
                C.ms("gp", f"ws{q}", ("wstr", i), C.E.dma_start(
                    out=wcat_d.ap()[:, i * 3072:(i + 1) * 3072],
                    in_=wstg_s[:, q * 3072:(q + 1) * 3072]), k=16)

        hist = []

        def ps_wait(eng):
            if len(hist) >= NPS:
                sem, key = hist[-NPS]
                C.wt(eng, sem, *key)
            else:
                C.wt(eng, "dve", "amemset", g2)
            return len(hist) % NPS

        def take_ps(eng, reader):
            ri = ps_wait(eng)
            hist.append(reader)
            return ri

        proj = mk_proj(C, pp, pS, take_ps) if with_proj else None

        pt_hist = []

        def pt_wait(eng):
            if len(pt_hist) >= 3:
                C.wt(eng, "pe", *pt_hist[-3])
            return len(pt_hist) % 3

        pt_slot = {}

        def em_S(s, hp, kb):
            key = (g2, s, hp, kb)
            ri = ps_wait("tensor")
            hist.append(("act", ("exp",) + key))
            if C.planning():
                C.ms("tensor", "pe", ("s", *key))
            elif C.on("tensor"):
                kA = kfm_s[0:64, hp * T + kb * 128:hp * T + (kb + 1) * 128]
                kB = kfm_s[64:128, hp * T + kb * 128:hp * T + (kb + 1) * 128]
                qA = qfm_s[0:64, hp * TH + s * 512:hp * TH + (s + 1) * 512]
                qB = qfm_s[64:128, hp * TH + s * 512:hp * TH + (s + 1) * 512]
                C.E.matmul(pS[ri][:, 0:512], kA, qA, start=True, stop=True,
                           tile_position=(0, 0))
                i = C.E.matmul(pS[ri][:, 512:1024], kB, qB,
                               start=True, stop=True, tile_position=(64, 0))
                C.ms("tensor", "pe", ("s", *key), i)
            return ri

        def em_mask(s, hp, kb, nkb):
            jj = kb - (nkb - 8)
            key = (g2, s, hp, kb)
            pi = pt_slot[key]
            C.wt("vector", "act", "exp", *key)
            C.wt("vector", "gld", "w_qkv0")
            if C.planning():
                C.ms("vector", "dve", ("mask", *key))
            elif C.on("vector"):
                mk = maskb_s[:, jj * 512:(jj + 1) * 512] \
                    .unsqueeze(1).broadcast_to([128, 2, 512])
                pv = pt_s[:, pi * 1024:(pi + 1) * 1024].rearrange(
                    "p (two f) -> p two f", two=2)
                i = C.E.tensor_tensor(pv, pv, mk, ALU.mult)
                C.ms("vector", "dve", ("mask", *key), i)

        def em_exp(s, hp, kb, ri, masked):
            key = (g2, s, hp, kb)
            C.wt("scalar", "pe", "s", *key)
            pi = pt_wait("scalar")
            pt_hist.append(("av",) + key)
            pt_slot[key] = pi
            if C.planning():
                C.ms("scalar", "act", ("exp", *key))
            elif C.on("scalar"):
                i = C.E.activation(pt_s[:, pi * 1024:(pi + 1) * 1024],
                                   pS[ri][:, 0:1024], AF.Exp,
                                   bias=F0, scale=np.float32(0.125))
                C.ms("scalar", "act", ("exp", *key), i)

        def em_AV(s, hp, kb, nkb):
            key = (g2, s, hp, kb)
            pi = pt_slot.pop(key)
            masked = kb >= nkb - 8
            if masked:
                C.wt("tensor", "dve", "mask", *key)
            else:
                C.wt("tensor", "act", "exp", *key)
            if kb == 0:
                prev = (g2, s, hp - 1) if hp > 0 else (
                    (g2, s - 1, 3) if s > 0 else
                    ((0, NSLOT - 1, 3) if g2 > 0 else None))
                if prev is not None:
                    C.wt("tensor", "act", "ocopy", *prev)
            if C.planning():
                C.ms("tensor", "pe", ("av", *key))
            elif C.on("tensor"):
                vA = vtm_s[:, kb * 520 + 2 * hp * 65:kb * 520 + 2 * hp * 65 + 65]
                vB = vtm_s[:, kb * 520 + (2 * hp + 1) * 65:
                           kb * 520 + (2 * hp + 1) * 65 + 65]
                C.E.matmul(pO[0:65, 0:512], vA,
                           pt_s[:, pi * 1024:pi * 1024 + 512],
                           start=(kb == 0), stop=(kb == nkb - 1),
                           skip_group_check=True)
                i = C.E.matmul(pO[0:65, 512:1024], vB,
                               pt_s[:, pi * 1024 + 512:(pi + 1) * 1024],
                               start=(kb == 0), stop=(kb == nkb - 1),
                               skip_group_check=True)
                C.ms("tensor", "pe", ("av", *key), i)

        def em_epi_a(s, hp, nkb):
            prev = (g2, s, hp - 1) if hp > 0 else (
                (g2, s - 1, 3) if s > 0 else
                ((0, NSLOT - 1, 3) if g2 > 0 else None))
            C.wt("scalar", "pe", "av", g2, s, hp, nkb - 1)
            if prev is not None:
                C.wt("scalar", "dve", "norm", *prev)
            if C.planning():
                C.ms("scalar", "act", ("ocopy", g2, s, hp))
            elif C.on("scalar"):
                i = C.E.activation(den_s[0:65, 0:1024], pO[0:65, 0:1024],
                                   AF.Identity, bias=F0, scale=F1)
                C.ms("scalar", "act", ("ocopy", g2, s, hp), i)

        def em_epi_b1(s, hp):
            if C.planning():
                C.ms("scalar", "act", ("rcp2", g2, s, hp))
            elif C.on("scalar"):
                C.E.drain()
                C.E.activation(den_s[64:65, 0:1024], den_s[64:65, 0:1024],
                               AF.Ln, bias=F0, scale=F1)
                C.E.drain()
                i = C.E.activation(rs_s[64:65, 0:1024], den_s[64:65, 0:1024],
                                   AF.Exp, bias=F0, scale=np.float32(-1.0))
                C.ms("scalar", "act", ("rcp2", g2, s, hp), i)

        def em_epi_b(s, hp):
            C.wt("tensor", "act", "rcp2", g2, s, hp)
            ri = ps_wait("tensor")
            hist.append(("dve", ("norm", g2, s, hp)))
            if C.planning():
                C.ms("tensor", "pe", ("bcast", g2, s, hp))
            elif C.on("tensor"):
                C.E.matmul(pS[ri][0:64, 0:512], ones_b[64:65, 0:64],
                           rs_s[64:65, 0:512], start=True, stop=True,
                           tile_position=(64, 0))
                i = C.E.matmul(pS[ri][0:64, 512:1024], ones_b[64:65, 0:64],
                               rs_s[64:65, 512:1024], start=True, stop=True,
                               tile_position=(64, 0))
                C.ms("tensor", "pe", ("bcast", g2, s, hp), i)
            C.wt("vector", "pe", "bcast", g2, s, hp)
            if hp == 0:
                if s > 0:
                    C.wt("vector", f"sto{(s - 1) % 2}", "ostore", g2, s - 1)
                elif g2 > 0:
                    C.wt("vector", f"sto{(NSLOT - 1) % 2}", "ostore",
                         0, NSLOT - 1)
            if C.planning():
                C.ms("vector", "dve", ("norm", g2, s, hp))
            elif C.on("vector"):
                C.E.drain()
                dst = stage_s[0:64, hp * 1024:(hp + 1) * 1024]
                i = C.E.tensor_tensor(dst, den_s[0:64, 0:1024],
                                      pS[ri][0:64, 0:1024], ALU.mult)
                C.ms("vector", "dve", ("norm", g2, s, hp), i)

        if g2 == 1 and stage >= 5:
            for i2 in range(24):
                em_bounce(i2)
        pending_epi = None
        for s in range(NSLOT):
            nkb = csched(s)
            for hp in range(4):
                for j in range(nkb + 2):
                    if j < nkb:
                        masked = j >= nkb - 8
                        ri = em_S(s, hp, j)
                        em_exp(s, hp, j, ri, masked)
                        if masked:
                            em_mask(s, hp, j, nkb)
                    if j == 2 and pending_epi is not None:
                        em_epi_b1(*pending_epi)
                    if j == 6 and pending_epi is not None:
                        em_epi_b(*pending_epi)
                        pending_epi = None
                    if (j in (2, 4, 6, 8, 10) and proj is not None
                            and s >= 1):
                        proj(max_tile=4 * s - 1)
                    if j >= 2:
                        em_AV(s, hp, j - 2, nkb)
                em_epi_a(s, hp, nkb)
                pending_epi = (s, hp)
            if pending_epi is not None and pending_epi[0] == s:
                em_epi_b1(*pending_epi)
                em_epi_b(*pending_epi)
                pending_epi = None
            for hp in range(4):
                C.wt("sync", "dve", "norm", g2, s, hp)
            if C.planning():
                C.ms("sync", f"sto{s % 2}", ("ostore", g2, s), k=32)
            elif C.on("sync"):
                ofv = ofm_s[:, :].rearrange("p (c f) -> p c f", c=8)
                stv = stage_s[0:64, :].rearrange("p (h two f) -> p h two f",
                                                 two=2, f=512)
                dstA = ofv[0:64, 4 * g2:4 * g2 + 4, s * 512:(s + 1) * 512]
                dstB = ofv[64:128, 4 * g2:4 * g2 + 4, s * 512:(s + 1) * 512]
                C.E.dma_start(out=dstA, in_=stv[:, :, 0, :]).then_inc(
                    C.sems[f"sto{s % 2}"], 16)
                i = C.E.dma_start(out=dstB, in_=stv[:, :, 1, :])
                C.ms("sync", f"sto{s % 2}", ("ostore", g2, s), i, k=16)
        if g2 == 1 and stage >= 5:
            C.snap("attn1_slots")
            if C.planning():
                C.ms("gp", "gld", ("w_g",), k=16)
            elif C.on("gp"):
                snapshot = C.vals[("snap", "attn1_slots")]
                for s_ in ["pe", "dve", "act"]:
                    C.E.wait_ge(C.sems[s_], snapshot[s_])
                wlast = (6, 7) if overlay else (22, 23)
                C.E.wait_ge(C.sems["ws0"], C.at("ws0", "wstr", wlast[0]))
                C.E.wait_ge(C.sems["ws1"], C.at("ws1", "wstr", wlast[1]))
                C.E.wait_ge(C.sems["gld"], C.at("gld", "w_proj"))
                C.ms("gp", "gld", ("w_g",), C.E.dma_start(
                    out=wg_s[:, :], in_=wcat_d.ap()[:, 0:24576]), k=16)
            if C.planning():
                C.ms("sync", "wub", ("w_u2",), k=16)
            elif C.on("sync"):
                snapshot = C.vals[("snap", "attn1_slots")]
                for s_ in ["pe", "dve", "act"]:
                    C.E.wait_ge(C.sems[s_], snapshot[s_])
                C.E.wait_ge(C.sems["ws0"], C.at("ws0", "wstr", 22))
                C.E.wait_ge(C.sems["ws1"], C.at("ws1", "wstr", 23))
                C.ms("sync", "wub", ("w_u2",), C.E.dma_start(
                    out=wu_s[:, :], in_=wcat_d.ap()[:, 24576:49152]), k=16)
        if proj is not None:
            for _ in range(3 * NQT):
                proj()
        if g2 == 1 and stage >= 5:
            C.snap("attn1_tail")
            if C.planning():
                C.ms("gp", "gld", ("w_d",), k=16)
            elif C.on("gp"):
                snapshot = C.vals[("snap", "attn1_tail")]
                for s_ in ["pe", "dve", "act", "x2st0", "x2st1",
                           "xo0", "xo1", "l2st0", "l2st1"]:
                    if snapshot.get(s_, 0) > 0:
                        C.E.wait_ge(C.sems[s_], snapshot[s_])
                C.E.wait_ge(C.sems["gld"], C.at("gld", "w_g"))
                C.E.wait_ge(C.sems["ws0"], C.at("ws0", "wstr", 22))
                C.E.wait_ge(C.sems["ws1"], C.at("ws1", "wstr", 23))
                C.ms("gp", "gld", ("w_d",), C.E.dma_start(
                    out=wd_s[:, :], in_=wcat_d.ap()[:, 49152:73728]), k=16)
        C.snap(f"attn{g2}_end")

    def mk_proj(C, pp, pS, take_ps):
        state = {"pos": 0}

        def em_front_a(t):
            p = t % 2
            if t >= 1:
                C.wt("sync", "dve", "x2", t - 1)
            if C.planning():
                C.ms("sync", f"xo{p}", ("xo", t), k=16)
            elif C.on("sync"):
                C.ms("sync", f"xo{p}", ("xo", t), C.E.dma_start(
                    out=xo_s[:, 0:DM],
                    in_=xown_d.ap()[t * 128:(t + 1) * 128, :]), k=16)

        def em_front(t):
            p = t % 2
            C.wt("tensor", "gld", "w_proj")
            sreq = t // 4
            C.wt("tensor", f"sto{sreq % 2}", "ostore", 1, sreq)
            if t >= 1:
                C.wt("tensor", "dve", "papply", t - 1)
            if C.planning():
                C.ms("tensor", "pe", ("proj", t))
            elif C.on("tensor"):
                for b in range(2):
                    C.E.matmul(pp[0][:, b * 512:(b + 1) * 512], ones_b[0:1, 0:128],
                               bproj_s[0:1, b * 512:(b + 1) * 512],
                               start=True, stop=False, skip_group_check=True)
                for c in range(DC):
                    lhsT = ofm_s[:, c * TH + t * 128:c * TH + (t + 1) * 128]
                    for b in range(2):
                        i = C.E.matmul(
                            pp[0][:, b * 512:(b + 1) * 512], lhsT,
                            wproj_s[:, c * DM + b * 512:c * DM + (b + 1) * 512],
                            start=False, stop=(c == DC - 1), skip_group_check=True)
                C.ms("tensor", "pe", ("proj", t), i)
            C.wt("vector", "pe", "proj", t)
            if t >= 2:
                C.wt("vector", "act", "l2rstd", t - 2)
            if C.planning():
                C.ms("vector", "dve", ("pstats", t))
            elif C.on("vector"):
                C.E.drain()
                C.E.bn_stats(bn2_s[:, p * 16:p * 16 + 6], pp[0][:, 0:512])
                C.E.bn_stats(bn2_s[:, p * 16 + 6:p * 16 + 12], pp[0][:, 512:1024])
                C.E.drain()
                i = C.E.bn_aggr(st2_s[:, p * 8:p * 8 + 2],
                                bn2_s[:, p * 16:p * 16 + 12])
                C.ms("vector", "dve", ("pstats", t), i)
            C.wt("scalar", "dve", "pstats", t)
            if C.planning():
                C.ms("scalar", "act", ("prstd", t))
            elif C.on("scalar"):
                C.E.activation(st2_s[:, p * 8 + 2:p * 8 + 3],
                               st2_s[:, p * 8 + 1:p * 8 + 2], AF.Ln,
                               bias=eps_s[:, 0:1], scale=F1)
                C.E.drain()
                i = C.E.activation(st2_s[:, p * 8 + 3:p * 8 + 4],
                                   st2_s[:, p * 8 + 2:p * 8 + 3], AF.Exp,
                                   bias=F0, scale=FH2)
                C.ms("scalar", "act", ("prstd", t), i)
            C.wt("vector", "act", "prstd", t)
            C.wt("vector", f"xo{p}", "xo", t)
            if t >= 2:
                C.wt("vector", f"x2st{p}", "x2st", t - 2)
            if C.planning():
                C.ms("vector", "dve", ("papply", t))
                C.ms("vector", "dve", ("x2", t))
            elif C.on("vector"):
                yb = ybuf_s[:, p * DM:(p + 1) * DM]
                i = C.E.tensor_scalar(yb, pp[0][:, 0:1024],
                                      st2_s[:, p * 8:p * 8 + 1],
                                      st2_s[:, p * 8 + 3:p * 8 + 4],
                                      ALU.subtract, ALU.mult)
                C.ms("vector", "dve", ("papply", t), i)
                C.E.drain()
                i = C.E.tensor_tensor(yb, yb, xo_s[:, 0:DM], ALU.add)
                C.ms("vector", "dve", ("x2", t), i)
            C.wt("sync", "dve", "x2", t)
            if C.planning():
                C.ms("sync", f"x2st{p}", ("x2st", t), k=(32 if debug else 16))
            elif C.on("sync"):
                if debug:
                    C.E.dma_start(
                        out=x2_d.ap()[t * 128:(t + 1) * 128, :],
                        in_=ybuf_s[:, p * DM:(p + 1) * DM]).then_inc(
                        C.sems[f"x2st{p}"], 16)
                C.ms("sync", f"x2st{p}", ("x2st", t), C.E.dma_start(
                    out=out_d.ap()[t * 128:(t + 1) * 128, :],
                    in_=ybuf_s[:, p * DM:(p + 1) * DM]), k=16)

        def em_B(t):
            p = t % 2
            if C.planning():
                C.ms("vector", "dve", ("l2stats", t))
            elif C.on("vector"):
                yb = ybuf_s[:, p * DM:(p + 1) * DM]
                C.E.drain()
                C.E.bn_stats(bn2_s[:, p * 16:p * 16 + 6], yb[:, 0:512])
                C.E.bn_stats(bn2_s[:, p * 16 + 6:p * 16 + 12], yb[:, 512:1024])
                C.E.drain()
                i = C.E.bn_aggr(st2_s[:, p * 8:p * 8 + 2],
                                bn2_s[:, p * 16:p * 16 + 12])
                C.ms("vector", "dve", ("l2stats", t), i)
            C.wt("scalar", "dve", "l2stats", t)
            if C.planning():
                C.ms("scalar", "act", ("l2rstd", t))
            elif C.on("scalar"):
                C.E.activation(st2_s[:, p * 8 + 2:p * 8 + 3],
                               st2_s[:, p * 8 + 1:p * 8 + 2], AF.Ln,
                               bias=eps_s[:, 0:1], scale=F1)
                C.E.drain()
                i = C.E.activation(st2_s[:, p * 8 + 3:p * 8 + 4],
                                   st2_s[:, p * 8 + 2:p * 8 + 3], AF.Exp,
                                   bias=F0, scale=FH2)
                C.ms("scalar", "act", ("l2rstd", t), i)

        def em_back_a(t):
            p = t % 2
            C.wt("vector", "act", "l2rstd", t)
            if t >= 1:
                C.wt("vector", "pe", "tp2", t - 1)
            if C.planning():
                C.ms("vector", "dve", ("l2apply", t))
            elif C.on("vector"):
                C.E.drain()
                i = C.E.tensor_scalar(ln2f_s[:, 0:DM],
                                      ybuf_s[:, p * DM:(p + 1) * DM],
                                      st2_s[:, p * 8:p * 8 + 1],
                                      st2_s[:, p * 8 + 3:p * 8 + 4],
                                      ALU.subtract, ALU.mult)
                C.ms("vector", "dve", ("l2apply", t), i)

        def em_back_b(t):
            p = t % 2
            C.wt("tensor", "dve", "l2apply", t)
            C.wt("tensor", "gld", "w_qkv0")
            ri2 = take_ps("tensor", ("dve", ("l2ev", t)))
            if C.planning():
                C.ms("tensor", "pe", ("tp2", t))
            elif C.on("tensor"):
                for c in range(8):
                    i = C.E.transpose(
                        pS[ri2][:, c * 128:(c + 1) * 128],
                        ln2f_s[:, c * 128:(c + 1) * 128],
                        identf_s[:, :])
                C.ms("tensor", "pe", ("tp2", t), i)
            C.wt("vector", "pe", "tp2", t)
            if t >= 2:
                C.wt("vector", f"l2st{p}", "l2st", t - 2)
            if C.planning():
                C.ms("vector", "dve", ("l2ev", t))
            elif C.on("vector"):
                C.E.drain()
                i = C.E.tensor_copy(ln2b_s[:, p * DM:(p + 1) * DM],
                                    pS[ri2][:, :])
                C.ms("vector", "dve", ("l2ev", t), i)
            C.wt("sync", "dve", "l2ev", t)
            if C.planning():
                C.ms("sync", f"l2st{p}", ("l2st", t), k=16)
            elif C.on("sync"):
                dst = ln2_d.ap().rearrange("(c p) t -> p c t", p=128)[
                    :, :, t * 128:(t + 1) * 128]
                C.ms("sync", f"l2st{p}", ("l2st", t), C.E.dma_start(
                    out=dst, in_=ln2b_s[:, p * DM:(p + 1) * DM].rearrange(
                        "p (c t) -> p c t", c=8)), k=16)

        def step(max_tile=None):
            i = state["pos"]
            t, k = divmod(i, 3)
            if t >= NQT:
                return
            if k == 0 and max_tile is not None and t > max_tile:
                return
            state["pos"] += 1
            if i == 0:
                em_front_a(0)
            if k == 0:
                em_front(t)
            elif k == 1:
                if t + 1 < NQT:
                    em_front_a(t + 1)
                em_B(t)
            else:
                em_back_a(t)
                em_back_b(t)
        return step

    def ph_ffn(C, pgu, pdn):
        C.fence("tensor", "attn1_end", ["dve", "act"])
        C.fence("scalar", "attn1_end", ["pe", "dve"])
        C.fence("vector", "attn1_end", ["pe", "act"])
        C.fence("sync", "attn1_end", ["pe", "dve", "sto0", "sto1",
                                      "x2st0", "x2st1", "l2st0", "l2st1"])
        def em_l2sl(sl):
            p = sl % 2
            if sl >= 2:
                C.wt("sync", "pe", "gu", sl - 2, HC // 2 - 1)
            if C.planning():
                C.ms("sync", f"l2sl{p}", ("l2sl", sl), k=16)
            elif C.on("sync"):
                src = ln2_d.ap().rearrange("(c p) t -> p c t", p=128)[
                    :, :, sl * 256:(sl + 1) * 256]
                C.ms("sync", f"l2sl{p}", ("l2sl", sl), C.E.dma_start(
                    out=ln2sl_s[:, p * 2048:(p + 1) * 2048].rearrange(
                        "p (c t) -> p c t", c=8), in_=src), k=16)

        def em_chain_a(sl2):
            for tt2 in range(2):
                C.wt("vector", "pe", "dn", sl2, HC // 2 - 1)
                if C.planning():
                    C.ms("vector", "dve", ("fstats", sl2, tt2))
                elif C.on("vector"):
                    C.E.drain()
                    C.E.bn_stats(bn3_s[:, tt2 * 16:tt2 * 16 + 6],
                                 pdn[tt2][:, 0:512])
                    C.E.bn_stats(bn3_s[:, tt2 * 16 + 6:tt2 * 16 + 12],
                                 pdn[tt2][:, 512:1024])
                    C.E.drain()
                    C.E.bn_aggr(st3_s[:, tt2 * 8:tt2 * 8 + 2],
                                bn3_s[:, tt2 * 16:tt2 * 16 + 12])
                    C.E.drain()
                    C.E.tensor_scalar(st3_s[:, tt2 * 8 + 2:tt2 * 8 + 3],
                                      st3_s[:, tt2 * 8 + 1:tt2 * 8 + 2],
                                      EPS_FFN, 0.0, ALU.add, ALU.bypass)
                    C.E.drain()
                    i = C.E.reciprocal(st3_s[:, tt2 * 8 + 4:tt2 * 8 + 5],
                                       st3_s[:, tt2 * 8 + 2:tt2 * 8 + 3])
                    C.ms("vector", "dve", ("fstats", sl2, tt2), i)
                C.wt("scalar", "dve", "fstats", sl2, tt2)
                if C.planning():
                    C.ms("scalar", "act", ("frstd", sl2, tt2))
                elif C.on("scalar"):
                    i = C.E.activation(st3_s[:, tt2 * 8 + 3:tt2 * 8 + 4],
                                       st3_s[:, tt2 * 8 + 4:tt2 * 8 + 5],
                                       AF.Sqrt, bias=F0, scale=F1)
                    C.ms("scalar", "act", ("frstd", sl2, tt2), i)

        def em_chain_b(sl2):
            for tt2 in range(2):
                t = sl2 * 2 + tt2
                C.wt("vector", "act", "frstd", sl2, tt2)
                if sl2 >= 1:
                    C.wt("vector", f"outst{tt2}", "outst", sl2 - 1, tt2)
                if C.planning():
                    C.ms("vector", "dve", ("fapply", sl2, tt2))
                elif C.on("vector"):
                    C.E.drain()
                    yf = yffn_s[:, tt2 * DM:(tt2 + 1) * DM]
                    i = C.E.tensor_scalar(yf, pdn[tt2][:, 0:1024],
                                          st3_s[:, tt2 * 8:tt2 * 8 + 1],
                                          st3_s[:, tt2 * 8 + 3:tt2 * 8 + 4],
                                          ALU.subtract, ALU.mult)
                    C.ms("vector", "dve", ("fapply", sl2, tt2), i)
                C.wt("gp", "dve", "fapply", sl2, tt2)
                C.wt("gp", f"x2st{t % 2}", "x2st", t)
                if C.planning():
                    C.ms("gp", f"outst{tt2}", ("outst", sl2, tt2), k=16)
                elif C.on("gp"):
                    C.ms("gp", f"outst{tt2}", ("outst", sl2, tt2),
                         C.E.dma_start(
                             out=out_d.ap()[t * 128:(t + 1) * 128, :],
                             in_=yffn_s[:, tt2 * DM:(tt2 + 1) * DM],
                             accum_op=ALU.add), k=16)

        em_l2sl(0)
        for sl in range(NSL + 1):
            p = sl % 2
            def em_gu(hcp):
                pg = hcp % 2
                C.wt("tensor", f"l2sl{p}", "l2sl", sl)
                C.wt("tensor", "gld", "w_g")
                if hcp >= 2:
                    C.wt("tensor", "dve", "hmul", sl, hcp - 2)
                elif sl > 0:
                    C.wt("tensor", "dve", "hmul", sl - 1, HC // 2 - 2 + hcp)
                if C.planning():
                    C.ms("tensor", "pe", ("gu", sl, hcp))
                elif C.on("tensor"):
                    for j in range(2):
                        hc = 2 * hcp + j
                        for c in range(DC):
                            C.E.matmul(
                                pgu[pg][:, j * 256:(j + 1) * 256],
                                wg_s[:, c * FH + hc * 128:c * FH + (hc + 1) * 128],
                                ln2sl_s[:, p * 2048 + c * 256:
                                        p * 2048 + (c + 1) * 256],
                                start=(c == 0), stop=(c == DC - 1),
                                skip_group_check=True)
                    C.wt("tensor", "wub", "w_u2")
                    for j in range(2):
                        hc = 2 * hcp + j
                        for c in range(DC):
                            i = C.E.matmul(
                                pgu[pg][:, 512 + j * 256:512 + (j + 1) * 256],
                                wu_s[:, c * FH + hc * 128:c * FH + (hc + 1) * 128],
                                ln2sl_s[:, p * 2048 + c * 256:
                                        p * 2048 + (c + 1) * 256],
                                start=(c == 0), stop=(c == DC - 1),
                                skip_group_check=True)
                    C.ms("tensor", "pe", ("gu", sl, hcp), i)
                C.wt("scalar", "pe", "gu", sl, hcp)
                if hcp >= 2:
                    C.wt("scalar", "dve", "hmul", sl, hcp - 2)
                elif sl > 0:
                    C.wt("scalar", "dve", "hmul", sl - 1, HC // 2 - 2 + hcp)
                if C.planning():
                    C.ms("scalar", "act", ("silu", sl, hcp))
                elif C.on("scalar"):
                    i = C.E.activation(gs_s[:, pg * 512:(pg + 1) * 512],
                                       pgu[pg][:, 0:512], AF.Silu,
                                       bias=F0, scale=np.float32(C127))
                    C.ms("scalar", "act", ("silu", sl, hcp), i)
                C.wt("vector", "act", "silu", sl, hcp)
                if hcp >= 3:
                    C.wt("vector", "pe", "dn", sl, hcp - 3)
                elif sl > 0:
                    C.wt("vector", "pe", "dn", sl - 1, HC // 2 - 3 + hcp)
                if C.planning():
                    C.ms("vector", "dve", ("hmul", sl, hcp))
                elif C.on("vector"):
                    p3 = hcp % 3
                    i = C.E.tensor_tensor(hfm_s[:, p3 * 512:(p3 + 1) * 512],
                                          gs_s[:, pg * 512:(pg + 1) * 512],
                                          pgu[pg][:, 512:1024], ALU.mult)
                    C.ms("vector", "dve", ("hmul", sl, hcp), i)

            def em_dn(hcp):
                C.wt("tensor", "dve", "hmul", sl, hcp)
                C.wt("tensor", "gld", "w_d")
                if hcp == 0 and sl >= 1:
                    for q2 in range(2):
                        C.wt("tensor", "dve", "fapply", sl - 1, q2)
                if C.planning():
                    C.ms("tensor", "pe", ("dn", sl, hcp))
                elif C.on("tensor"):
                    p3 = hcp % 3
                    for j in range(2):
                        hc = 2 * hcp + j
                        for tt2 in range(2):
                            lhsT = hfm_s[:, p3 * 512 + j * 256 + tt2 * 128:
                                         p3 * 512 + j * 256 + (tt2 + 1) * 128]
                            for b in range(2):
                                i = C.E.matmul(
                                    pdn[tt2][:, b * 512:(b + 1) * 512], lhsT,
                                    wd_s[:, hc * DM + b * 512:
                                         hc * DM + (b + 1) * 512],
                                    start=(hc == 0), stop=(hc == HC - 1),
                                    skip_group_check=True)
                    C.ms("tensor", "pe", ("dn", sl, hcp), i)

            for j in range(HC // 2 + 2):
                if sl < NSL and j < HC // 2:
                    em_gu(j)
                if j == 0 and sl >= 1:
                    em_chain_a(sl - 1)
                if j == 1 and sl >= 1:
                    em_chain_b(sl - 1)
                if sl < NSL and j >= 2:
                    em_dn(j - 2)
            if sl + 1 < NSL:
                em_l2sl(sl + 1)
        C.snap("ffn_end")

    def ph_final(C):
        if C.planning():
            C.snap("final")
            return
        if C.on("sync"):
            final = C.vals[("snap", "final")]
            for s in SEMS:
                v = final.get(s, 0)
                if v > 0:
                    C.E.wait_ge(C.sems[s], v)

    def ph_debug(C):
        if not debug:
            return
        taps = []
        if stage < 4:
            taps += [("dbg_qfm", qfm_s), ("dbg_kfm", kfm_s), ("dbg_vtm", vtm_s)]
        if 2 <= stage <= 4:
            taps += [("dbg_ofm", ofm_s)]
        if C.planning():
            for j in range(len(taps)):
                C.ms("gp", "dbg", ("dbg", j), k=16)
            return
        if C.on("gp"):
            names = {"dbg_qfm": dbg_qfm, "dbg_kfm": dbg_kfm,
                     "dbg_vtm": dbg_vtm, "dbg_ofm": dbg_ofm}
            final = C.vals[("snap", "final")]
            for s in ["pe", "dve", "act", "sto0", "sto1"]:
                v = final.get(s, 0)
                if v > 0:
                    C.E.wait_ge(C.sems[s], v)
            for j, (nm, src) in enumerate(taps):
                C.ms("gp", "dbg", ("dbg", j),
                     C.E.dma_start(out=names[nm].ap(), in_=src[:, :]), k=16)
            if taps:
                C.E.wait_ge(C.sems["dbg"], C.at("dbg", "dbg", len(taps) - 1))

    def walk(C, P):
        ph_loads(C)
        if stage >= 1:
            ph_qkv(C, 0, P.get("pq"), P.get("ptpl"), P.get("ptpq"))
        if stage >= 2:
            ph_attn(C, 0, P.get("pS"), P.get("pO"))
        if stage >= 3:
            ph_qkv(C, 1, P.get("pq"), P.get("ptpl"), P.get("ptpq"))
            ph_attn(C, 1, P.get("pS"), P.get("pO"), P.get("pp"),
                    with_proj=(stage >= 4))
        if stage >= 5:
            ph_ffn(C, P.get("pgu"), P.get("pdn"))
        ph_final(C)
        ph_debug(C)

    walk(C, {})

    es = ExitStack()
    sems = {n: es.enter_context(nc.semaphore("sem_" + n)) for n in SEMS}
    C.sems = sems
    ENGS = [("sync", "sync"), ("gp", "gpsimd"), ("tensor", "tensor"),
            ("vector", "vector"), ("scalar", "scalar")]

    with es, nc.Block() as block:
        def emit(fn, *args):
            for mode, attr in ENGS:
                C.mode = mode

                def body(E, fn=fn, args=args):
                    C.E = E
                    fn(C, *args)
                getattr(block, attr)(body)

        def qkv_psums(pes, tag):
            pq = [pes.enter_context(nc.psum_tensor(f"pq{i}_{tag}", [128, 1024], F32))
                  for i in range(3)]
            ptpl = pes.enter_context(nc.psum_tensor(f"ptpl_{tag}", [128, 1024], BF16))
            ptpq = pes.enter_context(nc.psum_tensor(f"ptpq_{tag}", [128, 512], BF16))
            return pq, ptpl, ptpq

        def attn_psums(pes, tag):
            pS = [pes.enter_context(nc.psum_tensor(f"pS{i}_{tag}", [128, 1024], F32))
                  for i in range(3)]
            pO = pes.enter_context(nc.psum_tensor(f"pO_{tag}", [128, 1024], F32))
            return pS, pO

        emit(ph_loads)
        if stage >= 1:
            with ExitStack() as pes:
                pq, ptpl, ptpq = qkv_psums(pes, "a")
                emit(ph_qkv, 0, pq, ptpl, ptpq)
        if stage >= 2:
            with ExitStack() as pes:
                pS, pO = attn_psums(pes, "a")
                emit(ph_attn, 0, pS, pO)
        if stage >= 3:
            with ExitStack() as pes:
                pq, ptpl, ptpq = qkv_psums(pes, "b")
                emit(ph_qkv, 1, pq, ptpl, ptpq)
            with ExitStack() as pes:
                nps = 2 if stage >= 4 else 3
                pS = [pes.enter_context(
                    nc.psum_tensor(f"pS{i}_b", [128, 1024], F32))
                    for i in range(nps)]
                pO = pes.enter_context(nc.psum_tensor("pO_b", [128, 1024], F32))
                pp = None
                if stage >= 4:
                    pp = [pes.enter_context(
                        nc.psum_tensor("pp0", [128, 1024], F32))]
                emit(ph_attn, 1, pS, pO, pp, stage >= 4)
        if stage >= 5:
            with ExitStack() as pes:
                pgu = [pes.enter_context(nc.psum_tensor(f"pgu{i}", [128, 1024],
                                                        F32)) for i in range(2)]
                pdn = [pes.enter_context(nc.psum_tensor(f"pdn{i}", [128, 1024],
                                                        F32)) for i in range(2)]
                emit(ph_ffn, pgu, pdn)
        emit(ph_final)
        emit(ph_debug)

    return nc


def _perm_cols():
    return np.concatenate([np.arange(0, D, 2), np.arange(1, D, 2)])


_TAB_CACHE = {}


def _rope_tables(T):
    if T in _TAB_CACHE:
        return _TAB_CACHE[T]
    try:
        import jax
        import jax.numpy as jnp
        with jax.default_device(jax.devices("cpu")[0]):
            pos = jnp.arange(T, dtype=jnp.float32)
            inv = 1.0 / 10000.0 ** (jnp.arange(0, D, 2, dtype=jnp.float32) / D)
            fr = pos[:, None] * inv[None, :]
            cos = np.asarray(jnp.cos(fr), dtype=np.float32)
            sin = np.asarray(jnp.sin(fr), dtype=np.float32)
    except Exception:
        posn = np.arange(T, dtype=np.float32)
        invn = (1.0 / 10000.0 ** (np.arange(0, D, 2, dtype=np.float32) / D)
                ).astype(np.float32)
        frn = posn[:, None] * invn[None, :]
        cos = np.cos(frn).astype(np.float32)
        sin = np.sin(frn).astype(np.float32)
    _TAB_CACHE[T] = (cos, sin)
    return cos, sin


def _rope_packed(positions, T_full):
    cos_f, sin_f = _rope_tables(T_full)
    cos = cos_f[positions]
    sin = sin_f[positions]
    return np.ascontiguousarray(np.concatenate([cos, cos, -sin, sin], axis=1))


def _mask_bank(z):
    tri = np.tril(np.full((128, 128), 1.0, dtype=np.float32), k=-1)
    bank = np.zeros((128, 8, 512), dtype=np.float32)
    for jj in range(8):
        if z == 0:
            if jj < 4:
                o = 128 * jj
                bank[:, jj, :o] = 1.0
                bank[:, jj, o:o + 128] = tri
            else:
                bank[:, jj, :] = 1.0
        else:
            if jj >= 4:
                o = 128 * (jj - 4)
                bank[:, jj, :o] = 1.0
                bank[:, jj, o:o + 128] = tri
    return np.ascontiguousarray(1.0 - bank.reshape(128, 8 * 512))


def _own_rows(z, T):
    blocks = np.arange(z, T // 512, 2)
    return (blocks[:, None] * 512 + np.arange(512)[None, :]).reshape(-1)


def make_in_maps(inputs, T=4096, n_cores=8):
    x = np.asarray(inputs["L0_attn_x"], dtype=np.float32)
    Wqkv = np.asarray(inputs["L0_attn_Wqkv"], dtype=np.float32)
    bqkv = np.asarray(inputs["L0_attn_bqkv"], dtype=np.float32)
    Wproj = np.asarray(inputs["L0_attn_Wproj"], dtype=np.float32)
    bproj = np.asarray(inputs["L0_attn_bproj"], dtype=np.float32)
    wg = np.asarray(inputs["L0_ffn_w_gate"]).astype(np.float32)
    wu = np.asarray(inputs["L0_ffn_w_up"]).astype(np.float32)
    wd = np.asarray(inputs["L0_ffn_w_down"]).astype(np.float32)

    perm = _perm_cols()
    cols = []
    for g in range(2):
        heads = np.arange(g * 8, (g + 1) * 8)
        qcols = np.concatenate([h * D + perm for h in heads])
        kcols = np.concatenate([DM + h * D + perm for h in heads])
        vcols = np.concatenate([2 * DM + h * D + np.arange(D) for h in heads])
        cols.append(np.concatenate([qcols, kcols, vcols]))
    cols = np.concatenate(cols)
    wqkv_c = np.ascontiguousarray(Wqkv[:, cols])
    bqkv_c = np.ascontiguousarray(bqkv[cols])[None, :]

    ropek = _rope_packed(np.arange(T), T)
    wgT = np.ascontiguousarray(wg.T)
    wuT = np.ascontiguousarray(wu.T)
    wdT = np.ascontiguousarray(wd.T)
    ident = np.eye(128, dtype=np.float32)

    in_maps = []
    for c in range(n_cores):
        b, z = c // 2, c % 2
        rows = _own_rows(z, T)
        in_maps.append({
            "x": np.ascontiguousarray(x[b]),
            "x_own": np.ascontiguousarray(x[b][rows]),
            "wqkv": wqkv_c, "bqkv": bqkv_c,
            "wproj": Wproj, "bproj": np.ascontiguousarray(bproj)[None, :],
            "wg": wgT, "wu": wuT, "wd": wdT,
            "ropek": ropek,
            "ropeq": np.ascontiguousarray(ropek[rows]),
            "maskb": _mask_bank(z),
            "ident": ident,
        })
    return in_maps


_NC_CACHE = {}


def kernel(**inputs):
    T = 4096
    n_cores = 8
    key = (T, n_cores)
    if key not in _NC_CACHE:
        _NC_CACHE[key] = build_nc(T=T, n_cores=n_cores, stage=5, debug=False)
    nc = _NC_CACHE[key]
    in_maps = make_in_maps(inputs, T=T, n_cores=n_cores)
    res = run_bass_kernel_spmd(nc, in_maps, core_ids=list(range(n_cores)))
    B = np.asarray(inputs["L0_attn_x"]).shape[0]
    out = np.empty((B, T, DM), dtype=np.float32)
    for c in range(n_cores):
        b, z = c // 2, c % 2
        out[b, _own_rows(z, T)] = res.results[c]["out"]
    return out
